# version 19
# baseline (speedup 1.0000x reference)
import numpy as np
import concourse.bass as bass
import concourse.mybir as mybir
from concourse.bass_utils import run_bass_kernel_spmd
from contextlib import ExitStack

F32 = mybir.dt.float32
BF16 = mybir.dt.bfloat16
ALU = mybir.AluOpType
ACT = mybir.ActivationFunctionType
AX = mybir.AxisListType
MAGIC = 12582912.0
EPS = 1e-5


def build_nc(D, T, H, NT):
    assert T == D
    KD = D // 128
    KT = T // 128
    KH = H // 128
    GH = H // NT
    GE = D // NT
    TB = T // 2
    KTB = TB // 128
    GB = TB // NT
    TS = min(4 * 128, TB)
    NSUB = TB // TS
    TH = T // 2
    KTH = TH // 128
    HH = H // 2
    WCH = min(1024, HH, D // 2)
    DH = D // 2
    SCALE = float(128.0 ** -0.5)
    inv_d = 1.0 / D

    nc = bass.Bass("TRN2", target_bir_lowering=False)

    x = nc.dram_tensor("x", [T, D], F32, kind="ExternalInput").ap()
    wq = nc.dram_tensor("wq", [KD, 128, D], F32, kind="ExternalInput").ap()
    wk = nc.dram_tensor("wk", [KD, 128, D], F32, kind="ExternalInput").ap()
    wv = nc.dram_tensor("wv", [KD, 128, D], F32, kind="ExternalInput").ap()
    wo = nc.dram_tensor("wo", [KD, 128, D], F32, kind="ExternalInput").ap()
    bqkv = nc.dram_tensor("bqkv", [128, 3 * KD], F32, kind="ExternalInput").ap()
    bo_rep = nc.dram_tensor("bo_rep", [128, D], F32, kind="ExternalInput").ap()
    wup = nc.dram_tensor("wup", [KD, 128, H], F32, kind="ExternalInput").ap()
    wdn = nc.dram_tensor("wdn", [KH, 128, D], F32, kind="ExternalInput").ap()
    psel = nc.dram_tensor("psel", [128, 8], F32, kind="ExternalInput").ap()
    out = nc.dram_tensor("out", [T, D], F32, kind="ExternalOutput").ap()

    d_xlnh = nc.dram_tensor("d_xlnh", [T, D], BF16, kind="Internal").ap()
    d_xlnl = nc.dram_tensor("d_xlnl", [T, D], BF16, kind="Internal").ap()
    d_wql = nc.dram_tensor("d_wql", [KD, 128, D], BF16, kind="Internal").ap()
    d_wkl = nc.dram_tensor("d_wkl", [KD, 128, D], BF16, kind="Internal").ap()
    d_wvl = nc.dram_tensor("d_wvl", [KD, 128, D], BF16, kind="Internal").ap()
    d_wol = nc.dram_tensor("d_wol", [KD, 128, D], BF16, kind="Internal").ap()
    d_kvs = nc.dram_tensor("d_kvs", [KD, 128, T], F32, kind="Internal").ap()
    d_ctxh = nc.dram_tensor("d_ctxh", [KD, 128, T], BF16, kind="Internal").ap()
    d_ctxl = nc.dram_tensor("d_ctxl", [KD, 128, T], BF16, kind="Internal").ap()
    d_r1 = nc.dram_tensor("d_r1", [T, D], F32, kind="Internal").ap()
    d_nup = nc.dram_tensor("d_nup", [T, D], BF16, kind="Internal").ap()
    d_relu = nc.dram_tensor("d_relu", [T, H], F32, kind="Internal").ap()
    d_ndn = nc.dram_tensor("d_ndn", [T, H], BF16, kind="Internal").ap()
    d_wupq = nc.dram_tensor("d_wupq", [KD, 128, H], BF16, kind="Internal").ap()
    d_wdnq = nc.dram_tensor("d_wdnq", [KH, 128, D], BF16, kind="Internal").ap()
    d_carry = nc.dram_tensor("d_carry", [128, KD], F32, kind="Internal").ap()
    d_call = nc.dram_tensor("d_call", [8, 128, KD], F32, kind="Internal").ap()

    ctx = ExitStack()

    def mk_sb(stack, name, shape, dt=F32):
        return stack.enter_context(nc.sbuf_tensor(name, shape, dt))

    def sb(name, shape, dt=F32):
        return mk_sb(ctx, name, shape, dt)

    s_sum = sb("s_sum", [128, KT]); s_ssq = sb("s_ssq", [128, KT])
    s_rstd = sb("s_rstd", [128, KT]); s_nmr = sb("s_nmr", [128, KT])
    s_sc1 = sb("s_sc1", [128, KT]); s_sc2 = sb("s_sc2", [128, KT])
    s_c2 = sb("s_c2", [128, KT]); s_s2 = sb("s_s2", [128, KT])
    s_maxr = sb("s_maxr", [128, KT]); s_k2 = sb("s_k2", [128, KT])
    s_q3 = sb("s_q3", [128, KT]); s_k3 = sb("s_k3", [128, KT])
    s_carry = sb("s_carry", [128, KD])
    s_carry0 = sb("s_carry0", [128, KD])
    s_crecv = sb("s_crecv", [128, 8 * KD])
    s_rcm = [sb(f"s_rcm{i}", [128, KD]) for i in range(2)]
    s_psel = sb("s_psel", [128, 8])
    s_bqkv = sb("s_bqkv", [128, 3 * KD])
    s_one = sb("s_one", [128, 1]); s_nmg = sb("s_nmg", [128, 1])
    s_row = sb("s_row", [1, 128])
    s_red = sb("s_red", [128, 1]); s_red2 = sb("s_red2", [128, 1])
    s_swu = sb("s_swu", [128, 1]); s_swd = sb("s_swd", [128, 1])
    s_iwu = sb("s_iwu", [128, 1]); s_iwd = sb("s_iwd", [128, 1])
    s_wsum = sb("s_wsum", [128, 2])

    g_pw = ExitStack()
    s_wfl = [mk_sb(g_pw, f"s_wfl{i}", [128, WCH]) for i in range(2)]
    s_wq8 = [mk_sb(g_pw, f"s_wq8{i}", [128, WCH], BF16) for i in range(2)]


    PSW = max(NT, 512)
    ps_a_t = [ctx.enter_context(nc.psum_tensor(f"ps_a{i}", [128, PSW], F32)) for i in range(4)]
    ps_b_t = [ctx.enter_context(nc.psum_tensor(f"ps_b{i}", [128, PSW], F32)) for i in range(2)]
    ps_o_t = [ctx.enter_context(nc.psum_tensor(f"ps_o{i}", [128, PSW], F32)) for i in range(2)]
    ps_a = [t[:, 0:NT] for t in ps_a_t]
    ps_b = [t[:, 0:NT] for t in ps_b_t]
    ps_o = [t[:, 0:NT] for t in ps_o_t]

    class Sem:
        __slots__ = ("h", "v")
        def __init__(self, name):
            self.h = ctx.enter_context(nc.semaphore(name))
            self.v = 0

    plan = {"sync": [], "gpsimd": [], "vector": [], "scalar": [], "tensor": []}

    def emit(eng, fn):
        plan[eng].append(fn)

    def wait(eng, sem, val):
        if val <= 0:
            return
        emit(eng, lambda e, h=sem.h, v=val: e.wait_ge(h, v))

    def dma(eng, sem, outap, inap, transpose=False, accum=False):
        sem.v += 16
        if transpose:
            emit(eng, lambda e, o=outap, i=inap, h=sem.h: e.dma_start(out=o, in_=i, transpose=True).then_inc(h, 16))
        elif accum:
            emit(eng, lambda e, o=outap, i=inap, h=sem.h: e.dma_start(out=o, in_=i, accum_op=ALU.add).then_inc(h, 16))
        else:
            emit(eng, lambda e, o=outap, i=inap, h=sem.h: e.dma_start(out=o, in_=i).then_inc(h, 16))
        return sem.v

    AUTOF = ("vector", "scalar", "gpsimd")

    def op(eng, sem, fn):
        sem.v += 1
        v = sem.v
        emit(eng, lambda e, f=fn, h=sem.h: f(e).then_inc(h, 1))
        if eng in AUTOF:
            emit(eng, lambda e, h=sem.h, v=v: e.wait_ge(h, v))
        return v

    def raw(eng, fn):
        emit(eng, fn)

    sv = Sem("sv"); sa = Sem("sa"); pe = Sem("pe")
    sg = Sem("sg"); sw = Sem("sw"); sr = Sem("sr"); so = Sem("so")
    scc = Sem("scc")
    sx = [Sem("sx0"), Sem("sx1")]
    sgk = [Sem("sgk0"), Sem("sgk1")]
    sgq = [Sem("sgq0"), Sem("sgq1")]
    sgo = [Sem("sgo0"), Sem("sgo1")]
    sgu = [Sem("sgu0"), Sem("sgu1")]
    sgd = [Sem("sgd0"), Sem("sgd1")]
    ql = [Sem("ql0"), Sem("ql1")]
    qsu = [Sem("qsu0"), Sem("qsu1")]
    qsd = [Sem("qsd0"), Sem("qsd1")]
    qsl = [Sem("qsl0"), Sem("qsl1")]
    swx = [Sem("swx0"), Sem("swx1")]
    swy = [Sem("swy0"), Sem("swy1")]
    swn = [Sem("swn0"), Sem("swn1")]
    swr = [Sem("swr0"), Sem("swr1")]
    swl = [Sem("swl0"), Sem("swl1")]
    swd = [Sem("swd0"), Sem("swd1")]
    srl = [Sem("srl0"), Sem("srl1")]
    srr = [Sem("srr0"), Sem("srr1")]
    sxs = [Sem("sxs0"), Sem("sxs1")]
    swc = Sem("swc"); src8 = Sem("src8"); srk = Sem("srk")
    sch = Sem("sch"); scl = Sem("scl"); scx = Sem("scx")
    so2 = [Sem("so20"), Sem("so21")]
    gq = Sem("gq")
    qlh = [Sem("qlh0"), Sem("qlh1")]

    lo_specs = []
    for kd2 in range(KD):
        lo_specs.append((wk, d_wkl, kd2))
        lo_specs.append((wv, d_wvl, kd2))
    for kd2 in range(KD):
        lo_specs.append((wq, d_wql, kd2))
    for kd2 in range(KD):
        lo_specs.append((wo, d_wol, kd2))

    def _chunks(kind):
        KN = KD if kind == "u" else KH
        NCOL = H if kind == "u" else D
        return [(kind, k, c) for k in range(KN) for c in range(0, NCOL, WCH)]

    tasks = []
    for ti in range(len(lo_specs)):
        for hf in range(2):
            for c0 in range(hf * DH, (hf + 1) * DH, WCH):
                tasks.append(("lo", ti, c0))
    for kind, k, c in _chunks("u") + _chunks("d"):
        tasks.append(("w1", kind, k, c))
    for kind, k, c in _chunks("u") + _chunks("d"):
        tasks.append(("w2", kind, k, c))

    n_lo_tasks = sum(1 for t in tasks if t[0] == "lo")
    n_lo_per_tile = n_lo_tasks // len(lo_specs)
    n_w1_tasks = sum(1 for t in tasks if t[0] == "w1")
    n_tasks_before_wu = n_lo_tasks + n_w1_tasks + sum(
        1 for t in tasks if t[0] == "w2" and t[1] == "u")
    t_issue = [0]
    t_cons = [0]
    cons_guard = [0, 0]
    whc_guard = [0, 0]
    w2_ready = [0]
    w1_guard = [0, 0]
    w2_a_guard = [0, 0]
    w2_w_guard = [0, 0]; w2_w_sem = [None, None]
    lo_wg = [0, 0]
    lo_wg_issue = [0, 0]
    lo_done_tile = {}
    load_done = {}

    def _src_ap(t):
        if t[0] == "lo":
            _, ti, c0 = t
            srcw, _, kd2 = lo_specs[ti]
            return srcw[kd2][:, c0:c0 + WCH]
        _, kind, k, c = t
        return (wup if kind == "u" else wdn)[k][:, c:c + WCH]

    def t_issue_one():
        i = t_issue[0]
        if i >= len(tasks):
            return
        t_issue[0] += 1
        p = i % 2
        t = tasks[i]
        wait("scalar", sv, cons_guard[p])
        v = dma("scalar", ql[p], s_wfl[p][:], _src_ap(t))
        entries = [(ql[p], v)]
        if t[0] == "lo":
            wait("gpsimd", sv, whc_guard[p])
            wait("gpsimd", qsl[p], lo_wg_issue[p])
            v2 = dma("gpsimd", qlh[p], s_wq8[p][:], _src_ap(t))
            entries.append((qlh[p], v2))
        load_done[i] = entries

    def t_consume_one():
        i = t_cons[0]
        if i >= len(tasks) or i >= t_issue[0]:
            return False
        t = tasks[i]
        if t[0] == "w2" and w2_ready[0] == 0:
            return False
        t_cons[0] += 1
        p = i % 2
        for sem_, val_ in load_done[i]:
            wait("vector", sem_, val_)
        if t[0] == "lo":
            _, ti, c0 = t
            _, dstw, kd2 = lo_specs[ti]
            wait("vector", qsl[p], lo_wg[p])
            op("vector", sv, lambda e, p=p: e.tensor_tensor(
                s_wq8[p][:], s_wfl[p][:], s_wq8[p][:], ALU.subtract))
            cons_guard[p] = sv.v
            whc_guard[p] = sv.v
            wait("scalar", sv, sv.v)
            wait("scalar", qsl[p], lo_wg[p])
            dma("scalar", qsl[p], dstw[kd2][:, c0:c0 + WCH], s_wq8[p][:])
            lo_wg[p] = qsl[p].v
            lo_wg_issue[p] = qsl[p].v
            lo_done_tile.setdefault(ti, []).append((qsl[p], qsl[p].v))
        elif t[0] == "w1":
            _, kind, k, c = t
            op("vector", sv, lambda e, p=p: e.tensor_reduce(
                s_red[:], s_wfl[p][:], AX.X, ALU.add, apply_absolute_value=True))
            col = 0 if kind == "u" else 1
            if k == 0 and c == 0:
                op("vector", sv, lambda e, col=col: e.tensor_copy(s_wsum[:, col:col + 1], s_red[:]))
            else:
                op("vector", sv, lambda e, col=col: e.tensor_tensor(
                    s_wsum[:, col:col + 1], s_wsum[:, col:col + 1], s_red[:], ALU.add))
            cons_guard[p] = sv.v
            w1_guard[p] = sv.v
        else:
            _, kind, k, c = t
            dstp = (d_wupq if kind == "u" else d_wdnq)[k][:, c:c + WCH]
            inv = s_iwu if kind == "u" else s_iwd
            wait("vector", sv, w2_ready[0])
            op("vector", sv, lambda e, p=p, inv=inv: e.tensor_scalar(
                s_wfl[p][:], s_wfl[p][:], inv[:, 0:1], 1.0, ALU.mult, ALU.min))
            op("vector", sv, lambda e, p=p: e.tensor_scalar(
                s_wfl[p][:], s_wfl[p][:], -1.0, MAGIC, ALU.max, ALU.add))
            wait("scalar", sv, sv.v)
            wait("scalar", sa, w2_a_guard[p])
            if w2_w_sem[p] is not None:
                wait("scalar", w2_w_sem[p], w2_w_guard[p])
            wait("scalar", qsl[p], lo_wg[p])
            op("scalar", sa, lambda e, p=p: e.activation(
                s_wq8[p][:], s_wfl[p][:], ACT.Identity, bias=s_nmg[:, 0:1]))
            w2_a_guard[p] = sa.v
            cons_guard[p] = sv.v
            wait("scalar", sa, w2_a_guard[p])
            qs = (qsu if kind == "u" else qsd)[p]
            dma("scalar", qs, dstp, s_wq8[p][:])
            w2_w_sem[p] = qs
            w2_w_guard[p] = qs.v
        return True

    def run_task(n=1):
        for _ in range(n):
            while t_issue[0] < min(len(tasks), t_cons[0] + 2):
                t_issue_one()
            t_consume_one()
        while t_issue[0] < min(len(tasks), t_cons[0] + 2):
            t_issue_one()

    def lo_task():
        run_task(1)

    def w1_task():
        run_task(1)

    def w2_task():
        run_task(1)

    g_p0 = ExitStack()
    s_x = [mk_sb(g_p0, f"s_x{i}", [128, D]) for i in range(2)]
    s_sq = [mk_sb(g_p0, f"s_sq{i}", [128, D]) for i in range(2)]
    s_xh = [mk_sb(g_p0, f"s_xh{i}", [128, D], BF16) for i in range(2)]
    s_xl = [mk_sb(g_p0, f"s_xl{i}", [128, D], BF16) for i in range(2)]

    dma("sync", sr, s_psel[:], psel)
    dma("sync", sr, s_bqkv[:], bqkv)
    op("vector", sv, lambda e: e.memset(s_nmg[:], -MAGIC))
    op("vector", sv, lambda e: e.memset(s_one[:], 1.0))
    op("vector", sv, lambda e: e.memset(s_row[:], 1.0))
    op("vector", sv, lambda e: e.memset(s_maxr[:], 0.0))

    stats_done = [0, 0]
    sq_done = [0, 0]
    for i in range(KT):
        p = i % 2
        wait("sync", sv, stats_done[p])
        wait("sync", sa, sq_done[p])
        dma("sync", sx[p], s_x[p][:], x[i * 128:(i + 1) * 128, :])
        wait("vector", sx[p], sx[p].v)
        wait("scalar", sx[p], sx[p].v)
        wait("scalar", sv, sq_done[p])
        op("scalar", sa, lambda e, p=p, i=i: e.activation(
            s_sq[p][:], s_x[p][:], ACT.Square, accum_out=s_ssq[:, i:i + 1]))
        sq_done[p] = sa.v
        op("vector", sv, lambda e, p=p, i=i: e.tensor_reduce(s_sum[:, i:i + 1], s_x[p][:], AX.X, ALU.add))
        stats_done[p] = sv.v
        lo_task()

    wait("vector", sa, sa.v)
    op("vector", sv, lambda e: e.tensor_scalar(s_sum[:], s_sum[:], inv_d, None, ALU.mult))
    op("vector", sv, lambda e: e.tensor_scalar(s_ssq[:], s_ssq[:], inv_d, None, ALU.mult))
    op("vector", sv, lambda e: e.tensor_tensor(s_sc1[:], s_sum[:], s_sum[:], ALU.mult))
    op("vector", sv, lambda e: e.tensor_tensor(s_sc1[:], s_ssq[:], s_sc1[:], ALU.subtract))
    op("vector", sv, lambda e: e.tensor_scalar(s_sc1[:], s_sc1[:], EPS, None, ALU.add))
    op("vector", sv, lambda e: e.reciprocal(s_sc2[:], s_sc1[:]))
    va_done = sv.v
    wait("scalar", sv, va_done)
    op("scalar", sa, lambda e: e.activation(s_rstd[:], s_sc2[:], ACT.Sqrt))
    wait("vector", sa, sa.v)

    def newton(dst_ap, va_ap, scr_ap, n=2):
        for _ in range(n):
            op("vector", sv, lambda e: e.tensor_tensor(scr_ap, dst_ap, dst_ap, ALU.mult))
            op("vector", sv, lambda e: e.tensor_tensor(scr_ap, scr_ap, va_ap, ALU.mult))
            op("vector", sv, lambda e: e.tensor_scalar(scr_ap, scr_ap, -0.5, 1.5, ALU.mult, ALU.add))
            op("vector", sv, lambda e: e.tensor_tensor(dst_ap, dst_ap, scr_ap, ALU.mult))

    newton(s_rstd[:], s_sc1[:], s_sc2[:])
    op("vector", sv, lambda e: e.tensor_tensor(s_nmr[:], s_sum[:], s_rstd[:], ALU.mult))
    op("vector", sv, lambda e: e.tensor_scalar(s_nmr[:], s_nmr[:], -1.0, None, ALU.mult))
    ln1_done = sv.v

    wait("scalar", sv, ln1_done)
    wait("sync", sv, ln1_done)
    apply_done = [0, 0]
    split_done = [0, 0]
    xlnw_done = [0, 0]
    xlnlw_done = [0, 0]
    for i in range(KT):
        p = i % 2
        wait("sync", sv, split_done[p])
        wait("sync", sx[p], sx[p].v)
        dma("sync", sx[p], s_x[p][:], x[i * 128:(i + 1) * 128, :])
        wait("scalar", sx[p], sx[p].v)
        wait("scalar", sv, split_done[p])
        op("scalar", sa, lambda e, p=p, i=i: e.activation(
            s_sq[p][:], s_x[p][:], ACT.Identity, bias=s_nmr[:, i:i + 1], scale=s_rstd[:, i:i + 1]))
        apply_done[p] = sa.v
        wait("vector", sa, apply_done[p])
        wait("vector", swx[p], xlnw_done[p])
        wait("vector", swy[p], xlnlw_done[p])
        op("vector", sv, lambda e, p=p: e.tensor_copy(s_xh[p][:], s_sq[p][:]))
        op("vector", sv, lambda e, p=p: e.tensor_tensor(s_xl[p][:], s_sq[p][:], s_xh[p][:], ALU.subtract))
        split_done[p] = sv.v
        wait("sync", sv, split_done[p])
        dma("sync", swx[p], d_xlnh[i * 128:(i + 1) * 128, :], s_xh[p][:])
        xlnw_done[p] = swx[p].v
        dma("sync", swy[p], d_xlnl[i * 128:(i + 1) * 128, :], s_xl[p][:])
        xlnlw_done[p] = swy[p].v
        lo_task()

    g_p0.close()
    g_a = ExitStack()
    s_Xh = mk_sb(g_a, "s_Xh", [128, KD * TB], BF16)
    s_Xl = mk_sb(g_a, "s_Xl", [128, KD * TB], BF16)
    s_wkh = [mk_sb(g_a, f"s_wkh{i}", [128, D], BF16) for i in range(2)]
    s_wkl = [mk_sb(g_a, f"s_wkl{i}", [128, D], BF16) for i in range(2)]
    s_wvh = [mk_sb(g_a, f"s_wvh{i}", [128, D], BF16) for i in range(2)]
    s_wvl = [mk_sb(g_a, f"s_wvl{i}", [128, D], BF16) for i in range(2)]
    b1 = mk_sb(g_a, "b1", [128, TB]); b2 = mk_sb(g_a, "b2", [128, TB])
    b3 = mk_sb(g_a, "b3", [128, TB])

    p0_v, p0_a = sv.v, sa.v
    wait("sync", sv, p0_v)
    wait("sync", sa, p0_a)
    wait("sync", swx[0], swx[0].v)
    wait("sync", swx[1], swx[1].v)
    wait("sync", swy[0], swy[0].v)
    wait("sync", swy[1], swy[1].v)
    wait("gpsimd", sv, p0_v)
    wait("gpsimd", sa, p0_a)
    wait("gpsimd", swx[0], swx[0].v)
    wait("gpsimd", swx[1], swx[1].v)
    wait("gpsimd", swy[0], swy[0].v)
    wait("gpsimd", swy[1], swy[1].v)
    wait("gpsimd", sx[0], sx[0].v)
    wait("gpsimd", sx[1], sx[1].v)
    wait("gpsimd", qsl[0], qsl[0].v)
    wait("gpsimd", qsl[1], qsl[1].v)

    def combos3(whi, wlo, Xh, Xl):
        return ((whi, Xh), (whi, Xl), (wlo, Xh))

    kv_w_done = [0, 0]
    kv_pe_done = [0, 0]
    a_bank_free = {}
    kvsw_done = 0
    xT_done = 0
    for tc in range(2):
        wait("sync", pe, kv_pe_done[0])
        wait("sync", pe, kv_pe_done[1])
        for kd in range(KD):
            dma("sync", sr, s_Xh[:, kd * TB:(kd + 1) * TB],
                d_xlnh[tc * TB:(tc + 1) * TB, kd * 128:(kd + 1) * 128], transpose=True)
            dma("sync", sr, s_Xl[:, kd * TB:(kd + 1) * TB],
                d_xlnl[tc * TB:(tc + 1) * TB, kd * 128:(kd + 1) * 128], transpose=True)
        xT_done = sr.v
        wait("tensor", sr, xT_done)
        for ke in range(KD):
            p = ke % 2
            wait("gpsimd", pe, kv_pe_done[p])
            for tix in (2 * ke, 2 * ke + 1):
                while len(lo_done_tile.get(tix, [])) < n_lo_per_tile:
                    run_task(1)
                for sem_, val_ in lo_done_tile[tix]:
                    wait("gpsimd", sem_, val_)
            dma("gpsimd", sgk[p], s_wkh[p][:], wk[ke])
            dma("gpsimd", sgk[p], s_wvh[p][:], wv[ke])
            dma("gpsimd", sgk[p], s_wkl[p][:], d_wkl[ke])
            dma("gpsimd", sgk[p], s_wvl[p][:], d_wvl[ke])
            kv_w_done[p] = sgk[p].v
            wait("tensor", sgk[p], kv_w_done[p])
            wait("scalar", sw, kvsw_done)
            for g in range(GB):
                if ("k", g % 2) in a_bank_free:
                    wait("tensor", sa, a_bank_free[("k", g % 2)])
            nmm = 0
            for wt, Xt in combos3(s_wkh[p], s_wkl[p], s_Xh, s_Xl):
                for kd in range(KD):
                    for g in range(GB):
                        nmm += 1
                        st = nmm <= GB
                        sp = nmm > 3 * KD * GB - GB
                        fn = (lambda e, wt=wt, Xt=Xt, kd=kd, g=g, st=st, sp=sp: e.matmul(
                            ps_a[g % 2], wt[:, kd * 128:(kd + 1) * 128],
                            Xt[:, kd * TB + g * NT: kd * TB + g * NT + NT],
                            start=st, stop=sp))
                        if nmm == 3 * KD * GB:
                            op("tensor", pe, fn)
                        else:
                            raw("tensor", fn)
            mm_done = pe.v
            for g in range(GB):
                bank = g % 2
                gs = slice(g * NT, g * NT + NT)
                wait("scalar", pe, mm_done)
                op("scalar", sa, lambda e, bank=bank, gs=gs, ke=ke: e.activation(
                    b1[:, gs], ps_a[bank], ACT.Identity, bias=s_bqkv[:, KD + ke:KD + ke + 1]))
                a_bank_free[("k", bank)] = sa.v
                wait("vector", sa, sa.v)
                op("vector", sv, lambda e, gs=gs: e.tensor_single_scalar(b2[:, gs], b1[:, gs], 0.0, ALU.min))
                wait("scalar", sv, sv.v)
                op("scalar", sa, lambda e, gs=gs: e.activation(b3[:, gs], b2[:, gs], ACT.Exp))
                wait("vector", sa, sa.v)
                op("vector", sv, lambda e, gs=gs: e.scalar_tensor_tensor(
                    b2[:, gs], b1[:, gs], 0.0, b3[:, gs], ALU.max, ALU.add))
            for g in range(GB):
                if ("v", 2 + g % 2) in a_bank_free:
                    wait("tensor", sv, a_bank_free[("v", 2 + g % 2)])
            nmm = 0
            for wt, Xt in combos3(s_wvh[p], s_wvl[p], s_Xh, s_Xl):
                for kd in range(KD):
                    for g in range(GB):
                        nmm += 1
                        st = nmm <= GB
                        sp = nmm > 3 * KD * GB - GB
                        fn = (lambda e, wt=wt, Xt=Xt, kd=kd, g=g, st=st, sp=sp: e.matmul(
                            ps_a[2 + g % 2], wt[:, kd * 128:(kd + 1) * 128],
                            Xt[:, kd * TB + g * NT: kd * TB + g * NT + NT],
                            start=st, stop=sp))
                        if nmm == 3 * KD * GB:
                            op("tensor", pe, fn)
                        else:
                            raw("tensor", fn)
            mm_done = pe.v
            for g in range(GB):
                bank = 2 + (g % 2)
                gs = slice(g * NT, g * NT + NT)
                wait("vector", pe, mm_done)
                op("vector", sv, lambda e, bank=bank, gs=gs, ke=ke: e.scalar_tensor_tensor(
                    b1[:, gs], ps_a[bank], s_bqkv[:, 2 * KD + ke:2 * KD + ke + 1], b2[:, gs],
                    ALU.add, ALU.mult))
                a_bank_free[("v", bank)] = sv.v
            wait("vector", sw, kvsw_done)
            if tc == 0:
                op("vector", sv, lambda e: e.tensor_tensor_scan(
                    b3[:], b1[:], b1[:], 0.0, ALU.add, ALU.bypass))
                op("vector", sv, lambda e, ke=ke: e.tensor_copy(s_carry0[:, ke:ke + 1], b3[:, TB - 1:TB]))
            else:
                op("vector", sv, lambda e, ke=ke: e.tensor_tensor_scan(
                    b3[:], b1[:], b1[:], s_carry0[:, ke:ke + 1], ALU.add, ALU.bypass))
                op("vector", sv, lambda e, ke=ke: e.tensor_copy(s_carry[:, ke:ke + 1], b3[:, TB - 1:TB]))
            scan_v = sv.v
            wait("sync", sv, scan_v)
            dma("sync", sw, d_kvs[ke][:, tc * TB:(tc + 1) * TB], b3[:])
            kvsw_done = sw.v
            kv_pe_done[p] = pe.v
            for _ in range(4):
                lo_task()
            for _ in range(2):
                w1_task()
    a_done_w = sw.v
    a_carry_v = sv.v
    while t_cons[0] < n_lo_tasks + n_w1_tasks:
        if not t_consume_one():
            break
        run_task(0)

    wait("sync", sv, a_carry_v)
    dma("sync", swc, d_carry, s_carry[:])
    wait("gpsimd", swc, swc.v)
    scc.v += 1
    emit("gpsimd", lambda e, h=scc.h: e.collective_compute(
        "AllGather", ALU.bypass, replica_groups=[[0, 1, 2, 3, 4, 5, 6, 7]],
        ins=[d_carry], outs=[d_call]).then_inc(h, 1))
    wait("sync", scc, scc.v)
    for r in range(8):
        dma("sync", src8, s_crecv[:, r * KD:(r + 1) * KD], d_call[r])
    wait("vector", src8, src8.v)
    op("vector", sv, lambda e: e.tensor_scalar(s_rcm[0][:], s_crecv[:, 0:KD], s_psel[:, 0:1], None, ALU.mult))
    for r in range(1, 8):
        cur, nxt = s_rcm[(r - 1) % 2], s_rcm[r % 2]
        op("vector", sv, lambda e, r=r, cur=cur, nxt=nxt: e.scalar_tensor_tensor(
            nxt[:], s_crecv[:, r * KD:(r + 1) * KD], s_psel[:, r:r + 1], cur[:], ALU.mult, ALU.add))
    s_rcmf = s_rcm[7 % 2]
    rcm_done = sv.v

    wait("tensor", sv, sv.v)
    op("tensor", pe, lambda e: e.matmul(ps_b_t[0][0:1, 0:1], s_wsum[:, 0:1], s_one[:], start=True, stop=True))
    op("tensor", pe, lambda e: e.matmul(ps_b_t[0][0:1, 1:2], s_wsum[:, 1:2], s_one[:], start=True, stop=True))
    t_ws = pe.v
    wait("vector", pe, t_ws)
    op("vector", sv, lambda e: e.tensor_copy(s_red[0:1, 0:1], ps_b_t[0][0:1, 0:1]))
    op("vector", sv, lambda e: e.tensor_copy(s_red2[0:1, 0:1], ps_b_t[0][0:1, 1:2]))
    tot_v = sv.v
    wait("tensor", sv, tot_v)
    op("tensor", pe, lambda e: e.matmul(ps_b_t[1][:, 0:1], s_row[:], s_red[0:1, 0:1], start=True, stop=True))
    op("tensor", pe, lambda e: e.matmul(ps_b_t[1][:, 1:2], s_row[:], s_red2[0:1, 0:1], start=True, stop=True))
    t_bc = pe.v
    wait("vector", pe, t_bc)
    op("vector", sv, lambda e: e.tensor_scalar(s_swu[:], ps_b_t[1][:, 0:1], 1.0 / (D * H), None, ALU.mult))
    op("vector", sv, lambda e: e.tensor_scalar(s_swd[:], ps_b_t[1][:, 1:2], 1.0 / (H * D), None, ALU.mult))
    op("vector", sv, lambda e: e.tensor_scalar(s_swu[:], s_swu[:], EPS, None, ALU.max))
    op("vector", sv, lambda e: e.tensor_scalar(s_swd[:], s_swd[:], EPS, None, ALU.max))
    op("vector", sv, lambda e: e.reciprocal(s_iwu[:], s_swu[:]))
    op("vector", sv, lambda e: e.reciprocal(s_iwd[:], s_swd[:]))
    w2_ready[0] = sv.v
    psb_used_v = sv.v

    g_a.close()
    g_b = ExitStack()
    s_Xh2 = mk_sb(g_b, "s_Xh2", [128, KD * TB], BF16)
    s_Xl2 = mk_sb(g_b, "s_Xl2", [128, KD * TB], BF16)

    av_, aa_ = sv.v, sa.v
    wait("sync", sv, av_)
    wait("sync", sa, aa_)
    wait("sync", sgk[0], sgk[0].v)
    wait("sync", sgk[1], sgk[1].v)
    wait("gpsimd", sv, av_)
    wait("gpsimd", sa, aa_)
    wait("gpsimd", pe, pe.v)
    wait("gpsimd", sw, a_done_w)
    wait("gpsimd", sr, sr.v)
    wait("gpsimd", qsl[0], qsl[0].v)
    wait("gpsimd", qsl[1], qsl[1].v)
    wait("tensor", sv, psb_used_v)
    sxb = [Sem("sxb0"), Sem("sxb1")]

    q_w_done = [0, 0]
    q_pe_done = [0, 0]
    b_bank_free = {}
    o_bank_free = {}
    kvt_done = 0
    wo_s_done = 0
    wo_guard = [0, 0]
    r1_guard = 0
    nup_a_done = 0
    nupw_prev = [0, 0]
    r1w_prev = [0, 0]
    xs_done = [0, 0]
    xs_guard = [0, 0]

    s_wqh = [mk_sb(g_b, f"s_wqh{i}", [128, D], BF16) for i in range(2)]
    s_wql = [mk_sb(g_b, f"s_wql{i}", [128, D], BF16) for i in range(2)]
    c1 = mk_sb(g_b, "c1", [128, TB]); c2t = mk_sb(g_b, "c2t", [128, TB])
    c3 = mk_sb(g_b, "c3", [128, TB])

    s_csh = mk_sb(g_b, "s_csh", [128, KD * TS], BF16)
    s_csl = mk_sb(g_b, "s_csl", [128, KD * TS], BF16)
    NTO = NT // 2
    s_woh = [mk_sb(g_b, f"s_woh{i}", [128, KD * NTO], BF16) for i in range(2)]
    s_wol0 = mk_sb(g_b, "s_wol0", [128, KD * NTO], BF16)
    s_wol = [s_wol0, s_wol0]
    s_r1s = [mk_sb(g_b, f"s_r1s{i}", [128, D]) for i in range(TS // 128)]
    s_xs = [mk_sb(g_b, "s_xs0", [128, NTO])] * 2
    s_sq3 = mk_sb(g_b, "s_sq3", [128, D])
    s_nup1 = mk_sb(g_b, "s_nup1", [128, D], BF16)
    _sq3b = s_sq3[:].bitcast(BF16)
    s_cth = _sq3b[:, 0:TB]
    s_ctl = _sq3b[:, TB:2 * TB]
    s_kvb = s_csh[:].bitcast(F32)[:, 0:TB]

    b_pe_mark = 0
    for tc in range(2):
        wait("sync", pe, b_pe_mark)
        for kd in range(KD):
            dma("sync", sr, s_Xh2[:, kd * TB:(kd + 1) * TB],
                d_xlnh[tc * TB:(tc + 1) * TB, kd * 128:(kd + 1) * 128], transpose=True)
            dma("sync", sr, s_Xl2[:, kd * TB:(kd + 1) * TB],
                d_xlnl[tc * TB:(tc + 1) * TB, kd * 128:(kd + 1) * 128], transpose=True)
        xT2_done = sr.v
        wait("tensor", sr, xT2_done)
        wait("vector", pe, pe.v)
        wait("vector", sa, sa.v)
        for ke in range(KD):
            p = ke % 2
            wait("gpsimd", pe, q_pe_done[p])
            while len(lo_done_tile.get(2 * KD + ke, [])) < n_lo_per_tile:
                run_task(1)
            for sem_, val_ in lo_done_tile[2 * KD + ke]:
                wait("gpsimd", sem_, val_)
            dma("gpsimd", sgq[p], s_wqh[p][:], wq[ke])
            dma("gpsimd", sgq[p], s_wql[p][:], d_wql[ke])
            q_w_done[p] = sgq[p].v
            wait("sync", sv, kvt_done)
            if ke == 0:
                wait("sync", pe, pe.v)
                wait("sync", sw, a_done_w)
            dma("sync", srk, s_kvb, d_kvs[ke][:, tc * TB:(tc + 1) * TB])
            kvs_rb = srk.v
            wait("tensor", sgq[p], q_w_done[p])
            wait("scalar", sv, sv.v)
            for g in range(GB):
                if (g % 2) in b_bank_free:
                    wait("tensor", sa, b_bank_free[g % 2])
            nmm = 0
            for wt, Xt in combos3(s_wqh[p], s_wql[p], s_Xh2, s_Xl2):
                for kd in range(KD):
                    for g in range(GB):
                        nmm += 1
                        st = nmm <= GB
                        sp = nmm > 3 * KD * GB - GB
                        fn = (lambda e, wt=wt, Xt=Xt, kd=kd, g=g, st=st, sp=sp: e.matmul(
                            ps_b[g % 2], wt[:, kd * 128:(kd + 1) * 128],
                            Xt[:, kd * TB + g * NT: kd * TB + g * NT + NT],
                            start=st, stop=sp))
                        if nmm == 3 * KD * GB:
                            op("tensor", pe, fn)
                        else:
                            raw("tensor", fn)
            mm_done = pe.v
            for g in range(GB):
                bank = g % 2
                gs = slice(g * NT, g * NT + NT)
                wait("scalar", pe, mm_done)
                op("scalar", sa, lambda e, bank=bank, gs=gs, ke=ke: e.activation(
                    c1[:, gs], ps_b[bank], ACT.Identity, bias=s_bqkv[:, ke:ke + 1]))
                b_bank_free[bank] = sa.v
                wait("vector", sa, sa.v)
                op("vector", sv, lambda e, gs=gs: e.tensor_single_scalar(c2t[:, gs], c1[:, gs], 0.0, ALU.min))
                wait("scalar", sv, sv.v)
                op("scalar", sa, lambda e, gs=gs: e.activation(c3[:, gs], c2t[:, gs], ACT.Exp))
                wait("vector", sa, sa.v)
                op("vector", sv, lambda e, gs=gs: e.scalar_tensor_tensor(
                    c2t[:, gs], c1[:, gs], 0.0, c3[:, gs], ALU.max, ALU.add))
            phiq_done = sv.v
            wait("vector", srk, kvs_rb)
            wait("vector", sv, rcm_done)
            op("vector", sv, lambda e, ke=ke: e.tensor_scalar(
                c1[:], s_kvb, s_rcmf[:, ke:ke + 1], None, ALU.add))
            op("vector", sv, lambda e: e.tensor_tensor(c3[:], c2t[:], c1[:], ALU.mult))
            wait("vector", sch, sch.v)
            wait("vector", scl, scl.v)
            op("vector", sv, lambda e: e.tensor_copy(s_cth, c3[:]))
            op("vector", sv, lambda e: e.tensor_tensor(s_ctl, c3[:], s_cth, ALU.subtract))
            kvt_done = sv.v
            wait("sync", sv, kvt_done)
            dma("sync", sch, d_ctxh[ke][:, tc * TB:(tc + 1) * TB], s_cth)
            dma("sync", scl, d_ctxl[ke][:, tc * TB:(tc + 1) * TB], s_ctl)
            q_pe_done[p] = pe.v
            for _ in range(2):
                w1_task()
        ctx_done = sv.v
        b_pe_mark = pe.v

        wait("tensor", sv, ctx_done)
        wait("vector", sch, sch.v)
        wait("vector", scl, scl.v)
        for sub in range(NSUB):
            t0 = tc * KTB + sub * (TS // 128)
            wait("vector", swr[0], swr[0].v)
            wait("vector", swr[1], swr[1].v)
            wait("sync", sch, sch.v)
            wait("sync", scl, scl.v)
            wait("sync", pe, wo_guard[0])
            for e in range(KD):
                dma("sync", scx, s_csh[:, e * TS:(e + 1) * TS],
                    d_ctxh[e][:, tc * TB + sub * TS: tc * TB + (sub + 1) * TS])
                dma("sync", scx, s_csl[:, e * TS:(e + 1) * TS],
                    d_ctxl[e][:, tc * TB + sub * TS: tc * TB + (sub + 1) * TS])
            ctxs_done = scx.v
            wait("tensor", scx, ctxs_done)
            for eo in range(2 * GE):
                pw = eo % 2
                wait("gpsimd", pe, wo_guard[pw])
                dma("gpsimd", sgo[pw], s_woh[pw][:],
                    wo[:, :, eo * NTO:(eo + 1) * NTO].rearrange("k p n -> p k n"))
                wait("gpsimd", pe, max(wo_guard))
                dma("gpsimd", sgo[pw], s_wol[pw][:],
                    d_wol[:, :, eo * NTO:(eo + 1) * NTO].rearrange("k p n -> p k n"))
                wo_s_done = sgo[pw].v
                wait("tensor", sgo[pw], wo_s_done)
                for ti in range(TS // 128):
                    t = t0 + ti
                    bank = ti % 2
                    if ("o", bank) in o_bank_free:
                        wait("tensor", sv, o_bank_free[("o", bank)])
                    first = True
                    for cstat, wmov in ((s_csh, s_woh[pw]), (s_csl, s_woh[pw]), (s_csh, s_wol[pw])):
                        for e in range(KD):
                            tloc = ti * 128
                            last = (wmov is s_wol[pw]) and e == KD - 1
                            fn = (lambda e_, cstat=cstat, wmov=wmov, e=e, tloc=tloc, bank=bank, st=first, sp=last: e_.matmul(
                                ps_o[bank][:, 0:NTO], cstat[:, e * TS + tloc: e * TS + tloc + 128],
                                wmov[:, e * NTO:(e + 1) * NTO], start=st, stop=sp))
                            if last:
                                op("tensor", pe, fn)
                            else:
                                raw("tensor", fn)
                            first = False
                    mm_done = pe.v
                    pxs = 0
                    wait("sync", sv, xs_guard[pxs])
                    dma("sync", sxs[pxs], s_xs[pxs][:], x[t * 128:(t + 1) * 128, eo * NTO:(eo + 1) * NTO])
                    xs_done[pxs] = sxs[pxs].v
                    wait("gpsimd", sxs[pxs], xs_done[pxs])
                    wait("gpsimd", sv, xs_guard[pxs])
                    dma("gpsimd", sxb[pxs], s_xs[pxs][:], bo_rep[:, eo * NTO:(eo + 1) * NTO], accum=True)
                    es = slice(eo * NTO, eo * NTO + NTO)
                    wait("vector", pe, mm_done)
                    wait("vector", sxb[pxs], sxb[pxs].v)
                    wait("vector", sv, r1_guard)
                    op("vector", sv, lambda e, bank=bank, es=es, ti=ti, pxs=pxs: e.scalar_tensor_tensor(
                        s_r1s[ti][:, es], ps_o[bank][:, 0:NTO], SCALE, s_xs[pxs][:], ALU.mult, ALU.add))
                    o_bank_free[("o", bank)] = sv.v
                    xs_guard[pxs] = sv.v
                wo_guard[pw] = pe.v
            for ti in range(TS // 128):
                t = t0 + ti
                tc2 = slice(t, t + 1)
                wait("vector", sa, nup_a_done)
                op("vector", sv, lambda e, ti=ti: e.tensor_tensor(s_sq3[:], s_r1s[ti][:], s_r1s[ti][:], ALU.mult))
                op("vector", sv, lambda e, tc2=tc2: e.tensor_reduce(s_ssq[:, tc2], s_sq3[:], AX.X, ALU.add))
                op("vector", sv, lambda e, ti=ti, tc2=tc2: e.tensor_reduce(s_sum[:, tc2], s_r1s[ti][:], AX.X, ALU.add))
                op("vector", sv, lambda e, tc2=tc2: e.tensor_scalar(s_sum[:, tc2], s_sum[:, tc2], inv_d, None, ALU.mult))
                op("vector", sv, lambda e, tc2=tc2: e.tensor_scalar(s_ssq[:, tc2], s_ssq[:, tc2], inv_d, None, ALU.mult))
                op("vector", sv, lambda e, tc2=tc2: e.tensor_tensor(s_sc1[:, tc2], s_sum[:, tc2], s_sum[:, tc2], ALU.mult))
                op("vector", sv, lambda e, tc2=tc2: e.tensor_tensor(s_sc1[:, tc2], s_ssq[:, tc2], s_sc1[:, tc2], ALU.subtract))
                op("vector", sv, lambda e, tc2=tc2: e.tensor_scalar(s_sc1[:, tc2], s_sc1[:, tc2], EPS, None, ALU.add))
                op("vector", sv, lambda e, tc2=tc2: e.reciprocal(s_sc2[:, tc2], s_sc1[:, tc2]))
                wait("scalar", sv, sv.v)
                op("scalar", sa, lambda e, tc2=tc2: e.activation(s_rstd[:, tc2], s_sc2[:, tc2], ACT.Sqrt))
                wait("vector", sa, sa.v)
                newton(s_rstd[:, tc2], s_sc1[:, tc2], s_sc2[:, tc2])
                op("vector", sv, lambda e, tc2=tc2: e.tensor_tensor(s_nmr[:, tc2], s_sum[:, tc2], s_rstd[:, tc2], ALU.mult))
                op("vector", sv, lambda e, tc2=tc2: e.tensor_scalar(s_nmr[:, tc2], s_nmr[:, tc2], -1.0, None, ALU.mult))
                wait("scalar", sv, sv.v)
                op("scalar", sa, lambda e, ti=ti, tc2=tc2: e.activation(
                    s_sq3[:], s_r1s[ti][:], ACT.Identity, bias=s_nmr[:, tc2], scale=s_rstd[:, tc2]))
                wait("vector", sa, sa.v)
                op("vector", sv, lambda e, tc2=tc2: e.tensor_reduce(
                    s_c2[:, tc2], s_sq3[:], AX.X, ALU.max, apply_absolute_value=True))
                op("vector", sv, lambda e, tc2=tc2: e.tensor_scalar(s_c2[:, tc2], s_c2[:, tc2], EPS, None, ALU.max))
                op("vector", sv, lambda e, tc2=tc2: e.reciprocal(s_sc2[:, tc2], s_c2[:, tc2]))
                op("vector", sv, lambda e, tc2=tc2: e.tensor_scalar(s_s2[:, tc2], s_sc2[:, tc2], 127.0, None, ALU.mult))
                op("vector", sv, lambda e, tc2=tc2: e.tensor_scalar(
                    s_sq3[:], s_sq3[:], s_s2[:, tc2], MAGIC, ALU.mult, ALU.add))
                wait("scalar", sv, sv.v)
                wait("scalar", swn[0], swn[0].v)
                op("scalar", sa, lambda e: e.activation(s_nup1[:], s_sq3[:], ACT.Identity, bias=s_nmg[:, 0:1]))
                nup_a_done = sa.v
                wait("sync", sa, nup_a_done)
                dma("sync", swn[0], d_nup[t * 128:(t + 1) * 128, :], s_nup1[:])
                wait("sync", sv, sv.v)
                dma("sync", swr[t % 2], d_r1[t * 128:(t + 1) * 128, :], s_r1s[ti][:])
                r1w_prev[t % 2] = swr[t % 2].v
                for _ in range(4):
                    w2_task()
            r1_guard = sv.v

    g_b.close()
    g_up = ExitStack()
    s_nupT = mk_sb(g_up, "s_nupT", [128, KD * T], BF16)
    s_wup = [mk_sb(g_up, f"s_wup{i}", [128, KD * NT], BF16) for i in range(2)]
    s_rel = [mk_sb(g_up, f"s_rel{i}", [128, NT]) for i in range(2)]

    o_v, o_a = sv.v, sa.v
    wait("sync", swn[0], swn[0].v)
    wait("sync", swn[1], swn[1].v)
    wait("sync", swr[0], swr[0].v)
    wait("sync", swr[1], swr[1].v)
    wait("sync", pe, pe.v)
    wait("sync", sv, o_v)
    wait("sync", sa, o_a)
    wait("sync", sxs[0], sxs[0].v)
    wait("sync", sxs[1], sxs[1].v)
    wait("gpsimd", sv, o_v)
    wait("gpsimd", sa, o_a)
    wait("gpsimd", pe, pe.v)
    wait("gpsimd", swr[0], swr[0].v)
    wait("gpsimd", swr[1], swr[1].v)
    for kd in range(KD):
        dma("sync", sr, s_nupT[:, kd * T:(kd + 1) * T],
            d_nup[:, kd * 128:(kd + 1) * 128], transpose=True)
    nupT_done = sr.v
    wait("tensor", sr, nupT_done)
    up_pe_done = [0, 0]
    upw_done = [0, 0]
    up_bank_free = {}
    relw_done = [0, 0]
    up_v_read = [0, 0]
    for h in range(GH):
        p = h % 2
        while t_cons[0] < n_tasks_before_wu:
            if not t_consume_one():
                break
            run_task(0)
        wait("gpsimd", qsu[0], qsu[0].v)
        wait("gpsimd", qsu[1], qsu[1].v)
        wait("gpsimd", pe, up_pe_done[p])
        for kd in range(KD):
            dma("gpsimd", sgu[p], s_wup[p][:, kd * NT:(kd + 1) * NT],
                d_wupq[kd][:, h * NT:(h + 1) * NT])
        upw_done[p] = sgu[p].v
        wait("tensor", sgu[p], upw_done[p])
        for t in range(KT):
            bank = t % 2
            if bank in up_bank_free:
                wait("tensor", sa, up_bank_free[bank])
            for kd in range(KD):
                fn = (lambda e, kd=kd, t=t, p=p, bank=bank: e.matmul(
                    ps_a[bank], s_nupT[:, kd * T + t * 128: kd * T + t * 128 + 128],
                    s_wup[p][:, kd * NT:(kd + 1) * NT],
                    start=(kd == 0), stop=(kd == KD - 1)))
                if kd == KD - 1:
                    op("tensor", pe, fn)
                else:
                    raw("tensor", fn)
            mm_done = pe.v
            wait("scalar", pe, mm_done)
            wait("scalar", swl[bank], relw_done[bank])
            wait("scalar", sv, up_v_read[bank])
            op("scalar", sa, lambda e, bank=bank: e.activation(s_rel[bank][:], ps_a[bank], ACT.Relu))
            up_bank_free[bank] = sa.v
            wait("vector", sa, sa.v)
            wait("vector", sv, sv.v)
            op("vector", sv, lambda e, bank=bank: e.tensor_reduce(s_red[:], s_rel[bank][:], AX.X, ALU.max))
            op("vector", sv, lambda e, t=t: e.tensor_tensor(s_maxr[:, t:t + 1], s_maxr[:, t:t + 1], s_red[:], ALU.max))
            up_v_read[bank] = sv.v
            wait("sync", sa, up_bank_free[bank])
            dma("sync", swl[bank], d_relu[t * 128:(t + 1) * 128, h * NT:(h + 1) * NT], s_rel[bank][:])
            relw_done[bank] = swl[bank].v
        up_pe_done[p] = pe.v
        for _ in range(8):
            w2_task()
    up_done_v = sv.v
    while t_cons[0] < len(tasks):
        if not t_consume_one():
            break
        run_task(0)

    wait("vector", sv, up_done_v)
    op("vector", sv, lambda e: e.tensor_scalar(s_k2[:], s_c2[:], s_swu[:, 0:1], 1.0 / 127.0, ALU.mult, ALU.mult))
    op("vector", sv, lambda e: e.tensor_tensor(s_sc1[:], s_maxr[:], s_k2[:], ALU.mult))
    op("vector", sv, lambda e: e.tensor_tensor(s_sc1[:], s_sc1[:], s_sc1[:], ALU.mult))
    op("vector", sv, lambda e: e.tensor_scalar(s_sc1[:], s_sc1[:], EPS, None, ALU.max))
    op("vector", sv, lambda e: e.reciprocal(s_sc2[:], s_sc1[:]))
    op("vector", sv, lambda e: e.tensor_tensor(s_q3[:], s_k2[:], s_k2[:], ALU.mult))
    op("vector", sv, lambda e: e.tensor_scalar(s_q3[:], s_q3[:], 127.0, None, ALU.mult))
    op("vector", sv, lambda e: e.tensor_tensor(s_q3[:], s_q3[:], s_sc2[:], ALU.mult))
    op("vector", sv, lambda e: e.tensor_scalar(s_k3[:], s_sc1[:], s_swd[:, 0:1], 1.0 / 127.0, ALU.mult, ALU.mult))
    q3_done = sv.v

    g_up.close()
    g_pw.close()
    g_nd = ExitStack()
    s_rlb = [mk_sb(g_nd, f"s_rlb{i}", [128, HH]) for i in range(2)]
    s_sqh = [mk_sb(g_nd, f"s_sqh{i}", [128, HH]) for i in range(2)]
    s_ndn = [mk_sb(g_nd, f"s_ndn{i}", [128, HH], BF16) for i in range(2)]
    g_dn = ExitStack()
    s_ndT = mk_sb(g_dn, "s_ndT", [128, KH * TH], BF16)
    s_wdn = [mk_sb(g_dn, f"s_wdn{i}", [128, KH * NT], BF16) for i in range(2)]
    s_r1L = [mk_sb(g_dn, f"s_r1L{i}", [128, NT]) for i in range(2)]
    s_o = [mk_sb(g_dn, f"s_o{i}", [128, NT]) for i in range(2)]

    up_a = sa.v
    wait("sync", pe, pe.v)
    wait("sync", swl[0], swl[0].v)
    wait("sync", swl[1], swl[1].v)
    wait("sync", sa, up_a)
    wait("sync", sv, sv.v)
    wait("sync", ql[0], ql[0].v)
    wait("sync", ql[1], ql[1].v)
    wait("sync", qsu[0], qsu[0].v)
    wait("sync", qsu[1], qsu[1].v)
    wait("sync", qsd[0], qsd[0].v)
    wait("sync", qsd[1], qsd[1].v)
    wait("sync", qsl[0], qsl[0].v)
    wait("sync", qsl[1], qsl[1].v)
    wait("sync", sgu[0], sgu[0].v)
    wait("sync", sgu[1], sgu[1].v)
    nd_v_done = [0, 0]
    nd_a_done = [0, 0]
    nd_w_done = [0, 0]
    nd_idx = [0]

    nd_issue = [0]
    nd_rb = {}

    def nd_issue_one(seq):
        k = nd_issue[0]
        if k >= len(seq):
            return
        nd_issue[0] += 1
        t, hhf = seq[k]
        pk = k % 2
        wait("sync", sv, nd_v_done[pk])
        wait("sync", sa, nd_a_done[pk])
        dma("sync", srl[pk], s_rlb[pk][:], d_relu[t * 128:(t + 1) * 128, hhf * HH:(hhf + 1) * HH])
        nd_rb[k] = srl[pk].v

    def nd_tiles(trange):
        seq_all = [(t, hhf) for t in range(KT) for hhf in range(2)]
        for t in trange:
            for hhf in range(2):
                k = nd_idx[0]
                p = k % 2
                nd_idx[0] += 1
                while nd_issue[0] <= min(len(seq_all) - 1, k + 1):
                    nd_issue_one(seq_all)
                rb = nd_rb[k]
                wait("vector", srl[p], rb)
                wait("vector", sv, q3_done)
                wait("vector", sa, nd_a_done[p])
                op("vector", sv, lambda e, p=p: e.tensor_tensor(s_sqh[p][:], s_rlb[p][:], s_rlb[p][:], ALU.mult))
                op("vector", sv, lambda e, p=p, t=t: e.tensor_scalar(
                    s_sqh[p][:], s_sqh[p][:], s_q3[:, t:t + 1], MAGIC, ALU.mult, ALU.add))
                nd_v_done[p] = sv.v
                wait("scalar", sv, nd_v_done[p])
                wait("scalar", swd[p], nd_w_done[p])
                op("scalar", sa, lambda e, p=p: e.activation(s_ndn[p][:], s_sqh[p][:], ACT.Identity, bias=s_nmg[:, 0:1]))
                nd_a_done[p] = sa.v
                wait("sync", sa, nd_a_done[p])
                dma("sync", swd[p], d_ndn[t * 128:(t + 1) * 128, hhf * HH:(hhf + 1) * HH], s_ndn[p][:])
                nd_w_done[p] = swd[p].v

    dn_pe_done = [0, 0]
    dn_o_guard = [0, 0]
    dn_r_guard = [0, 0]
    dn_bank_free = {}

    def down_chunk(tcc):
        wait("sync", swd[0], swd[0].v)
        wait("sync", swd[1], swd[1].v)
        wait("sync", swr[0], swr[0].v)
        wait("sync", swr[1], swr[1].v)
        wait("sync", pe, pe.v)
        wait("sync", sv, sv.v)
        wait("sync", sa, sa.v)
        for kh in range(KH):
            dma("sync", sr, s_ndT[:, kh * TH:(kh + 1) * TH],
                d_ndn[tcc * TH:(tcc + 1) * TH, kh * 128:(kh + 1) * 128], transpose=True)
        ndT_done = sr.v
        wait("tensor", sr, ndT_done)
        wait("gpsimd", sv, sv.v)
        wait("gpsimd", sa, sa.v)
        wait("gpsimd", pe, pe.v)
        for e in range(GE):
            pwe = e % 2
            wait("gpsimd", qsd[0], qsd[0].v)
            wait("gpsimd", qsd[1], qsd[1].v)
            wait("gpsimd", pe, dn_pe_done[pwe])
            for kh in range(KH):
                dma("gpsimd", sgd[pwe], s_wdn[pwe][:, kh * NT:(kh + 1) * NT],
                    d_wdnq[kh][:, e * NT:(e + 1) * NT])
            dnw = sgd[pwe].v
            wait("tensor", sgd[pwe], dnw)
            for ti in range(KTH):
                t = tcc * KTH + ti
                bank = 2 + (ti % 2)
                if bank in dn_bank_free:
                    wait("tensor", sv, dn_bank_free[bank])
                for kh in range(KH):
                    fn = (lambda e_, kh=kh, ti=ti, pwe=pwe, bank=bank: e_.matmul(
                        ps_a[bank], s_ndT[:, kh * TH + ti * 128: kh * TH + ti * 128 + 128],
                        s_wdn[pwe][:, kh * NT:(kh + 1) * NT],
                        start=(kh == 0), stop=(kh == KH - 1)))
                    if kh == KH - 1:
                        op("tensor", pe, fn)
                    else:
                        raw("tensor", fn)
                mm_done = pe.v
                pp = ti % 2
                wait("sync", sv, dn_r_guard[pp])
                dma("sync", srr[pp], s_r1L[pp][:], d_r1[t * 128:(t + 1) * 128, e * NT:(e + 1) * NT])
                r1rb = srr[pp].v
                wait("vector", srr[pp], r1rb)
                wait("vector", pe, mm_done)
                wait("vector", so2[pp], dn_o_guard[pp])
                op("vector", sv, lambda e_, bank=bank, pp=pp, t=t: e_.scalar_tensor_tensor(
                    s_o[pp][:], ps_a[bank], s_k3[:, t:t + 1], s_r1L[pp][:], ALU.mult, ALU.add))
                dn_bank_free[bank] = sv.v
                dn_r_guard[pp] = sv.v
                wait("sync", sv, sv.v)
                dma("sync", so2[pp], out[t * 128:(t + 1) * 128, e * NT:(e + 1) * NT], s_o[pp][:])
                dn_o_guard[pp] = so2[pp].v
            dn_pe_done[pwe] = pe.v

    up_pe_total = pe.v
    wait("gpsimd", pe, up_pe_total)
    nd_tiles(range(0, KTH))
    down_chunk(0)
    nd_tiles(range(KTH, KT))
    down_chunk(1)
    wait("sync", so2[0], so2[0].v)
    wait("sync", so2[1], so2[1].v)
    g_dn.close()
    g_nd.close()
    wait("sync", so, so.v)
    g_dn.close()

    with ctx:
        block = ctx.enter_context(nc.Block())

        @block.sync
        def _(eng):
            for f in plan["sync"]:
                f(eng)

        @block.gpsimd
        def _(eng):
            for f in plan["gpsimd"]:
                f(eng)

        @block.vector
        def _(eng):
            for f in plan["vector"]:
                f(eng)

        @block.scalar
        def _(eng):
            for f in plan["scalar"]:
                f(eng)

        @block.tensor
        def _(eng):
            for f in plan["tensor"]:
                f(eng)

    return nc


def _lay_proj(W):
    E, D = W.shape
    KD = D // 128
    return np.ascontiguousarray(W.T.reshape(KD, 128, E))


def _lay_stat(W):
    E, D = W.shape
    KE, KD = E // 128, D // 128
    A = W.reshape(KE, 128, KD, 128)
    return np.ascontiguousarray(A.transpose(0, 3, 2, 1).reshape(KE, 128, D))


def make_inputs(x_full, Wq, bq, Wk, bk, Wv, bv, Wo, bo, w_up, w_down, T):
    D = Wq.shape[0]
    KD = D // 128
    wq_l = _lay_stat(np.asarray(Wq, np.float32))
    wk_l = _lay_stat(np.asarray(Wk, np.float32))
    wv_l = _lay_stat(np.asarray(Wv, np.float32))
    wo_l = _lay_proj(np.asarray(Wo, np.float32))
    wup_l = _lay_proj(np.asarray(w_up, np.float32))
    wdn_l = _lay_proj(np.asarray(w_down, np.float32))
    bqkv = np.concatenate([np.asarray(bq, np.float32).reshape(KD, 128).T,
                           np.asarray(bk, np.float32).reshape(KD, 128).T,
                           np.asarray(bv, np.float32).reshape(KD, 128).T], axis=1)
    bo_r = np.ascontiguousarray(np.broadcast_to(
        np.asarray(bo, np.float32)[None, :], (128, D)))
    maps = []
    for c in range(8):
        ps = np.zeros((128, 8), np.float32)
        if c % 2 == 1:
            ps[:, c - 1] = 1.0
        maps.append({
            "x": np.ascontiguousarray(x_full[c * T:(c + 1) * T]),
            "wq": wq_l, "wk": wk_l, "wv": wv_l, "wo": wo_l,
            "bqkv": np.ascontiguousarray(bqkv), "bo_rep": bo_r,
            "wup": wup_l, "wdn": wdn_l, "psel": ps,
        })
    return maps


def kernel(L0_attn_x, L0_attn_Wq, L0_attn_bq, L0_attn_Wk, L0_attn_bk,
           L0_attn_Wv, L0_attn_bv, L0_attn_Wo, L0_attn_bo,
           L0_ffn_w_up, L0_ffn_w_down):
    B, L, D = L0_attn_x.shape
    T = (B * L) // 8
    H = L0_ffn_w_up.shape[0]
    x_full = np.asarray(L0_attn_x, dtype=np.float32).reshape(B * L, D)
    nc = build_nc(D, T, H, NT=512)
    in_maps = make_inputs(x_full, L0_attn_Wq, L0_attn_bq, L0_attn_Wk, L0_attn_bk,
                          L0_attn_Wv, L0_attn_bv, L0_attn_Wo, L0_attn_bo,
                          L0_ffn_w_up, L0_ffn_w_down, T)
    res = run_bass_kernel_spmd(nc, in_maps, core_ids=list(range(8)))
    outs = [np.asarray(res.results[c]["out"]) for c in range(8)]
    return np.concatenate(outs, axis=0).reshape(B, L, D).astype(np.float32)


# version 20 (final)
# speedup vs baseline: 1.0938x; 1.0938x over previous
import numpy as np
import concourse.bass as bass
import concourse.mybir as mybir
from concourse.bass_utils import run_bass_kernel_spmd
from contextlib import ExitStack

F32 = mybir.dt.float32
BF16 = mybir.dt.bfloat16
ALU = mybir.AluOpType
ACT = mybir.ActivationFunctionType
AX = mybir.AxisListType
MAGIC = 12582912.0
EPS = 1e-5


def build_nc(D, T, H, NT):
    assert T == D
    KD = D // 128
    KT = T // 128
    KH = H // 128
    GH = H // NT
    GE = D // NT
    TB = T // 2
    KTB = TB // 128
    GB = TB // NT
    TS = min(4 * 128, TB)
    NSUB = TB // TS
    TH = T // 2
    KTH = TH // 128
    HH = H // 2
    WCH = min(1024, HH, D // 2)
    DH = D // 2
    SCALE = float(128.0 ** -0.5)
    inv_d = 1.0 / D

    nc = bass.Bass("TRN2", target_bir_lowering=False)

    x = nc.dram_tensor("x", [T, D], F32, kind="ExternalInput").ap()
    wq = nc.dram_tensor("wq", [KD, 128, D], F32, kind="ExternalInput").ap()
    wk = nc.dram_tensor("wk", [KD, 128, D], F32, kind="ExternalInput").ap()
    wv = nc.dram_tensor("wv", [KD, 128, D], F32, kind="ExternalInput").ap()
    wo = nc.dram_tensor("wo", [KD, 128, D], F32, kind="ExternalInput").ap()
    bqkv = nc.dram_tensor("bqkv", [128, 3 * KD], F32, kind="ExternalInput").ap()
    bo_rep = nc.dram_tensor("bo_rep", [128, D], F32, kind="ExternalInput").ap()
    wup = nc.dram_tensor("wup", [KD, 128, H], F32, kind="ExternalInput").ap()
    wdn = nc.dram_tensor("wdn", [KH, 128, D], F32, kind="ExternalInput").ap()
    psel = nc.dram_tensor("psel", [128, 8], F32, kind="ExternalInput").ap()
    out = nc.dram_tensor("out", [T, D], F32, kind="ExternalOutput").ap()

    d_xlnh = nc.dram_tensor("d_xlnh", [T, D], BF16, kind="Internal").ap()
    d_xlnl = nc.dram_tensor("d_xlnl", [T, D], BF16, kind="Internal").ap()
    d_wql = nc.dram_tensor("d_wql", [KD, 128, D], BF16, kind="Internal").ap()
    d_wkl = nc.dram_tensor("d_wkl", [KD, 128, D], BF16, kind="Internal").ap()
    d_wvl = nc.dram_tensor("d_wvl", [KD, 128, D], BF16, kind="Internal").ap()
    d_wol = nc.dram_tensor("d_wol", [KD, 128, D], BF16, kind="Internal").ap()
    d_kvs = nc.dram_tensor("d_kvs", [KD, 128, T], F32, kind="Internal").ap()
    d_ctxh = nc.dram_tensor("d_ctxh", [KD, 128, T], BF16, kind="Internal").ap()
    d_ctxl = nc.dram_tensor("d_ctxl", [KD, 128, T], BF16, kind="Internal").ap()
    d_r1 = nc.dram_tensor("d_r1", [T, D], F32, kind="Internal").ap()
    d_nup = nc.dram_tensor("d_nup", [T, D], BF16, kind="Internal").ap()
    d_relu = nc.dram_tensor("d_relu", [T, H], F32, kind="Internal").ap()
    d_ndn = nc.dram_tensor("d_ndn", [T, H], BF16, kind="Internal").ap()
    d_wupq = nc.dram_tensor("d_wupq", [KD, 128, H], BF16, kind="Internal").ap()
    d_wdnq = nc.dram_tensor("d_wdnq", [KH, 128, D], BF16, kind="Internal").ap()
    d_carry = nc.dram_tensor("d_carry", [128, KD], F32, kind="Internal").ap()
    d_call = nc.dram_tensor("d_call", [8, 128, KD], F32, kind="Internal").ap()

    ctx = ExitStack()

    def mk_sb(stack, name, shape, dt=F32):
        return stack.enter_context(nc.sbuf_tensor(name, shape, dt))

    def sb(name, shape, dt=F32):
        return mk_sb(ctx, name, shape, dt)

    s_sum = sb("s_sum", [128, KT]); s_ssq = sb("s_ssq", [128, KT])
    s_rstd = sb("s_rstd", [128, KT]); s_nmr = sb("s_nmr", [128, KT])
    s_sc1 = sb("s_sc1", [128, KT]); s_sc2 = sb("s_sc2", [128, KT])
    s_c2 = sb("s_c2", [128, KT]); s_s2 = sb("s_s2", [128, KT])
    s_maxr = sb("s_maxr", [128, KT]); s_k2 = sb("s_k2", [128, KT])
    s_q3 = sb("s_q3", [128, KT]); s_k3 = sb("s_k3", [128, KT])
    s_carry = sb("s_carry", [128, KD])
    s_carry0 = sb("s_carry0", [128, KD])
    s_crecv = sb("s_crecv", [128, 8 * KD])
    s_rcm = [sb(f"s_rcm{i}", [128, KD]) for i in range(2)]
    s_psel = sb("s_psel", [128, 8])
    s_bqkv = sb("s_bqkv", [128, 3 * KD])
    s_one = sb("s_one", [128, 1]); s_nmg = sb("s_nmg", [128, 1])
    s_row = sb("s_row", [1, 128])
    s_red = sb("s_red", [128, 1]); s_red2 = sb("s_red2", [128, 1])
    s_swu = sb("s_swu", [128, 1]); s_swd = sb("s_swd", [128, 1])
    s_iwu = sb("s_iwu", [128, 1]); s_iwd = sb("s_iwd", [128, 1])
    s_wsum = sb("s_wsum", [128, 2])

    g_pw = ExitStack()
    s_wfl = [mk_sb(g_pw, f"s_wfl{i}", [128, WCH]) for i in range(2)]
    s_wq8 = [mk_sb(g_pw, f"s_wq8{i}", [128, WCH], BF16) for i in range(2)]


    PSW = max(NT, 512)
    ps_a_t = [ctx.enter_context(nc.psum_tensor(f"ps_a{i}", [128, PSW], F32)) for i in range(4)]
    ps_b_t = [ctx.enter_context(nc.psum_tensor(f"ps_b{i}", [128, PSW], F32)) for i in range(2)]
    ps_o_t = [ctx.enter_context(nc.psum_tensor(f"ps_o{i}", [128, PSW], F32)) for i in range(2)]
    ps_a = [t[:, 0:NT] for t in ps_a_t]
    ps_b = [t[:, 0:NT] for t in ps_b_t]
    ps_o = [t[:, 0:NT] for t in ps_o_t]

    class Sem:
        __slots__ = ("h", "v")
        def __init__(self, name):
            self.h = ctx.enter_context(nc.semaphore(name))
            self.v = 0

    plan = {"sync": [], "gpsimd": [], "vector": [], "scalar": [], "tensor": []}

    def emit(eng, fn):
        plan[eng].append(fn)

    def wait(eng, sem, val):
        if val <= 0:
            return
        emit(eng, lambda e, h=sem.h, v=val: e.wait_ge(h, v))

    def dma(eng, sem, outap, inap, transpose=False, accum=False):
        sem.v += 16
        if transpose:
            emit(eng, lambda e, o=outap, i=inap, h=sem.h: e.dma_start(out=o, in_=i, transpose=True).then_inc(h, 16))
        elif accum:
            emit(eng, lambda e, o=outap, i=inap, h=sem.h: e.dma_start(out=o, in_=i, accum_op=ALU.add).then_inc(h, 16))
        else:
            emit(eng, lambda e, o=outap, i=inap, h=sem.h: e.dma_start(out=o, in_=i).then_inc(h, 16))
        return sem.v

    AUTOF = ("vector", "scalar", "gpsimd")

    def op(eng, sem, fn):
        sem.v += 1
        v = sem.v
        emit(eng, lambda e, f=fn, h=sem.h: f(e).then_inc(h, 1))
        if eng in AUTOF:
            emit(eng, lambda e, h=sem.h, v=v: e.wait_ge(h, v))
        return v

    def raw(eng, fn):
        emit(eng, fn)

    sv = Sem("sv"); sa = Sem("sa"); pe = Sem("pe")
    sg = Sem("sg"); sw = Sem("sw"); sr = Sem("sr"); so = Sem("so")
    scc = Sem("scc")
    sx = [Sem("sx0"), Sem("sx1")]
    sgk = [Sem("sgk0"), Sem("sgk1")]
    sgq = [Sem("sgq0"), Sem("sgq1")]
    sgo = [Sem("sgo0"), Sem("sgo1")]
    sgu = [Sem("sgu0"), Sem("sgu1")]
    sgd = [Sem("sgd0"), Sem("sgd1")]
    ql = [Sem("ql0"), Sem("ql1")]
    qsu = [Sem("qsu0"), Sem("qsu1")]
    qsd = [Sem("qsd0"), Sem("qsd1")]
    qsl = [Sem("qsl0"), Sem("qsl1")]
    swx = [Sem("swx0"), Sem("swx1")]
    swy = [Sem("swy0"), Sem("swy1")]
    swn = [Sem("swn0"), Sem("swn1")]
    swr = [Sem("swr0"), Sem("swr1")]
    swl = [Sem("swl0"), Sem("swl1")]
    swd = [Sem("swd0"), Sem("swd1")]
    srl = [Sem("srl0"), Sem("srl1")]
    srr = [Sem("srr0"), Sem("srr1")]
    sxs = [Sem("sxs0"), Sem("sxs1")]
    swc = Sem("swc"); src8 = Sem("src8"); srk = Sem("srk")
    sch = Sem("sch"); scl = Sem("scl"); scx = Sem("scx")
    so2 = [Sem("so20"), Sem("so21")]
    gq = Sem("gq")
    qlh = [Sem("qlh0"), Sem("qlh1")]

    lo_specs = []
    for kd2 in range(KD):
        lo_specs.append((wk, d_wkl, kd2))
        lo_specs.append((wv, d_wvl, kd2))
    for kd2 in range(KD):
        lo_specs.append((wq, d_wql, kd2))
    for kd2 in range(KD):
        lo_specs.append((wo, d_wol, kd2))

    def _chunks(kind):
        KN = KD if kind == "u" else KH
        NCOL = H if kind == "u" else D
        return [(kind, k, c) for k in range(KN) for c in range(0, NCOL, WCH)]

    tasks = []
    for ti in range(len(lo_specs)):
        for hf in range(2):
            for c0 in range(hf * DH, (hf + 1) * DH, WCH):
                tasks.append(("lo", ti, c0))
    for kind, k, c in _chunks("u") + _chunks("d"):
        tasks.append(("w1", kind, k, c))
    for kind, k, c in _chunks("u") + _chunks("d"):
        tasks.append(("w2", kind, k, c))

    n_lo_tasks = sum(1 for t in tasks if t[0] == "lo")
    n_lo_per_tile = n_lo_tasks // len(lo_specs)
    n_w1_tasks = sum(1 for t in tasks if t[0] == "w1")
    n_tasks_before_wu = n_lo_tasks + n_w1_tasks + sum(
        1 for t in tasks if t[0] == "w2" and t[1] == "u")
    t_issue = [0]
    t_cons = [0]
    cons_guard = [0, 0]
    whc_guard = [0, 0]
    w2_ready = [0]
    w1_guard = [0, 0]
    w2_a_guard = [0, 0]
    w2_w_guard = [0, 0]; w2_w_sem = [None, None]
    lo_wg = [0, 0]
    lo_wg_issue = [0, 0]
    lo_done_tile = {}
    load_done = {}

    def _src_ap(t):
        if t[0] == "lo":
            _, ti, c0 = t
            srcw, _, kd2 = lo_specs[ti]
            return srcw[kd2][:, c0:c0 + WCH]
        _, kind, k, c = t
        return (wup if kind == "u" else wdn)[k][:, c:c + WCH]

    def t_issue_one():
        i = t_issue[0]
        if i >= len(tasks):
            return
        t_issue[0] += 1
        p = i % 2
        t = tasks[i]
        wait("sync", sv, cons_guard[p])
        v = dma("sync", ql[p], s_wfl[p][:], _src_ap(t))
        load_done[i] = [(ql[p], v)]

    def t_consume_one():
        i = t_cons[0]
        if i >= len(tasks) or i >= t_issue[0]:
            return False
        t = tasks[i]
        if t[0] == "w2" and w2_ready[0] == 0:
            return False
        t_cons[0] += 1
        p = i % 2
        for sem_, val_ in load_done[i]:
            wait("vector", sem_, val_)
        if t[0] == "lo":
            _, ti, c0 = t
            _, dstw, kd2 = lo_specs[ti]
            wait("vector", qsl[p], lo_wg[p])
            op("vector", sv, lambda e, p=p: e.tensor_copy(s_wq8[p][:], s_wfl[p][:]))
            op("vector", sv, lambda e, p=p: e.tensor_tensor(
                s_wq8[p][:], s_wfl[p][:], s_wq8[p][:], ALU.subtract))
            cons_guard[p] = sv.v
            whc_guard[p] = sv.v
            wait("sync", sv, sv.v)
            wait("sync", qsl[p], lo_wg[p])
            dma("sync", qsl[p], dstw[kd2][:, c0:c0 + WCH], s_wq8[p][:])
            lo_wg[p] = qsl[p].v
            lo_done_tile.setdefault(ti, []).append((qsl[p], qsl[p].v))
        elif t[0] == "w1":
            _, kind, k, c = t
            op("vector", sv, lambda e, p=p: e.tensor_reduce(
                s_red[:], s_wfl[p][:], AX.X, ALU.add, apply_absolute_value=True))
            col = 0 if kind == "u" else 1
            if k == 0 and c == 0:
                op("vector", sv, lambda e, col=col: e.tensor_copy(s_wsum[:, col:col + 1], s_red[:]))
            else:
                op("vector", sv, lambda e, col=col: e.tensor_tensor(
                    s_wsum[:, col:col + 1], s_wsum[:, col:col + 1], s_red[:], ALU.add))
            cons_guard[p] = sv.v
            w1_guard[p] = sv.v
        else:
            _, kind, k, c = t
            dstp = (d_wupq if kind == "u" else d_wdnq)[k][:, c:c + WCH]
            inv = s_iwu if kind == "u" else s_iwd
            wait("vector", sv, w2_ready[0])
            op("vector", sv, lambda e, p=p, inv=inv: e.tensor_scalar(
                s_wfl[p][:], s_wfl[p][:], inv[:, 0:1], 1.0, ALU.mult, ALU.min))
            op("vector", sv, lambda e, p=p: e.tensor_scalar(
                s_wfl[p][:], s_wfl[p][:], -1.0, MAGIC, ALU.max, ALU.add))
            wait("scalar", sv, sv.v)
            wait("scalar", sa, w2_a_guard[p])
            if w2_w_sem[p] is not None:
                wait("scalar", w2_w_sem[p], w2_w_guard[p])
            wait("scalar", qsl[p], lo_wg[p])
            op("scalar", sa, lambda e, p=p: e.activation(
                s_wq8[p][:], s_wfl[p][:], ACT.Identity, bias=s_nmg[:, 0:1]))
            w2_a_guard[p] = sa.v
            cons_guard[p] = sv.v
            wait("sync", sa, w2_a_guard[p])
            qs = (qsu if kind == "u" else qsd)[p]
            dma("sync", qs, dstp, s_wq8[p][:])
            w2_w_sem[p] = qs
            w2_w_guard[p] = qs.v
        return True

    def run_task(n=1):
        for _ in range(n):
            while t_issue[0] < min(len(tasks), t_cons[0] + 2):
                t_issue_one()
            t_consume_one()
        while t_issue[0] < min(len(tasks), t_cons[0] + 2):
            t_issue_one()

    def lo_task():
        run_task(1)

    def w1_task():
        run_task(1)

    def w2_task():
        run_task(1)

    g_p0 = ExitStack()
    s_x = [mk_sb(g_p0, f"s_x{i}", [128, D]) for i in range(2)]
    s_sq = [mk_sb(g_p0, f"s_sq{i}", [128, D]) for i in range(2)]
    s_xh = [mk_sb(g_p0, f"s_xh{i}", [128, D], BF16) for i in range(2)]
    s_xl = [mk_sb(g_p0, f"s_xl{i}", [128, D], BF16) for i in range(2)]

    dma("sync", sr, s_psel[:], psel)
    dma("sync", sr, s_bqkv[:], bqkv)
    op("vector", sv, lambda e: e.memset(s_nmg[:], -MAGIC))
    op("vector", sv, lambda e: e.memset(s_one[:], 1.0))
    op("vector", sv, lambda e: e.memset(s_row[:], 1.0))
    op("vector", sv, lambda e: e.memset(s_maxr[:], 0.0))

    stats_done = [0, 0]
    sq_done = [0, 0]
    for i in range(KT):
        p = i % 2
        wait("sync", sv, stats_done[p])
        wait("sync", sa, sq_done[p])
        dma("sync", sx[p], s_x[p][:], x[i * 128:(i + 1) * 128, :])
        wait("vector", sx[p], sx[p].v)
        wait("scalar", sx[p], sx[p].v)
        wait("scalar", sv, sq_done[p])
        op("scalar", sa, lambda e, p=p, i=i: e.activation(
            s_sq[p][:], s_x[p][:], ACT.Square, accum_out=s_ssq[:, i:i + 1]))
        sq_done[p] = sa.v
        op("vector", sv, lambda e, p=p, i=i: e.tensor_reduce(s_sum[:, i:i + 1], s_x[p][:], AX.X, ALU.add))
        stats_done[p] = sv.v
        lo_task()

    wait("vector", sa, sa.v)
    op("vector", sv, lambda e: e.tensor_scalar(s_sum[:], s_sum[:], inv_d, None, ALU.mult))
    op("vector", sv, lambda e: e.tensor_scalar(s_ssq[:], s_ssq[:], inv_d, None, ALU.mult))
    op("vector", sv, lambda e: e.tensor_tensor(s_sc1[:], s_sum[:], s_sum[:], ALU.mult))
    op("vector", sv, lambda e: e.tensor_tensor(s_sc1[:], s_ssq[:], s_sc1[:], ALU.subtract))
    op("vector", sv, lambda e: e.tensor_scalar(s_sc1[:], s_sc1[:], EPS, None, ALU.add))
    op("vector", sv, lambda e: e.reciprocal(s_sc2[:], s_sc1[:]))
    va_done = sv.v
    wait("scalar", sv, va_done)
    op("scalar", sa, lambda e: e.activation(s_rstd[:], s_sc2[:], ACT.Sqrt))
    wait("vector", sa, sa.v)

    def newton(dst_ap, va_ap, scr_ap, n=2):
        for _ in range(n):
            op("vector", sv, lambda e: e.tensor_tensor(scr_ap, dst_ap, dst_ap, ALU.mult))
            op("vector", sv, lambda e: e.tensor_tensor(scr_ap, scr_ap, va_ap, ALU.mult))
            op("vector", sv, lambda e: e.tensor_scalar(scr_ap, scr_ap, -0.5, 1.5, ALU.mult, ALU.add))
            op("vector", sv, lambda e: e.tensor_tensor(dst_ap, dst_ap, scr_ap, ALU.mult))

    newton(s_rstd[:], s_sc1[:], s_sc2[:])
    op("vector", sv, lambda e: e.tensor_tensor(s_nmr[:], s_sum[:], s_rstd[:], ALU.mult))
    op("vector", sv, lambda e: e.tensor_scalar(s_nmr[:], s_nmr[:], -1.0, None, ALU.mult))
    ln1_done = sv.v

    wait("scalar", sv, ln1_done)
    wait("sync", sv, ln1_done)
    apply_done = [0, 0]
    split_done = [0, 0]
    xlnw_done = [0, 0]
    xlnlw_done = [0, 0]
    for i in range(KT):
        p = i % 2
        wait("sync", sv, split_done[p])
        wait("sync", sx[p], sx[p].v)
        dma("sync", sx[p], s_x[p][:], x[i * 128:(i + 1) * 128, :])
        wait("scalar", sx[p], sx[p].v)
        wait("scalar", sv, split_done[p])
        op("scalar", sa, lambda e, p=p, i=i: e.activation(
            s_sq[p][:], s_x[p][:], ACT.Identity, bias=s_nmr[:, i:i + 1], scale=s_rstd[:, i:i + 1]))
        apply_done[p] = sa.v
        wait("vector", sa, apply_done[p])
        wait("vector", swx[p], xlnw_done[p])
        wait("vector", swy[p], xlnlw_done[p])
        op("vector", sv, lambda e, p=p: e.tensor_copy(s_xh[p][:], s_sq[p][:]))
        op("vector", sv, lambda e, p=p: e.tensor_tensor(s_xl[p][:], s_sq[p][:], s_xh[p][:], ALU.subtract))
        split_done[p] = sv.v
        wait("sync", sv, split_done[p])
        dma("sync", swx[p], d_xlnh[i * 128:(i + 1) * 128, :], s_xh[p][:])
        xlnw_done[p] = swx[p].v
        dma("sync", swy[p], d_xlnl[i * 128:(i + 1) * 128, :], s_xl[p][:])
        xlnlw_done[p] = swy[p].v
        lo_task()

    g_p0.close()
    g_a = ExitStack()
    s_Xh = mk_sb(g_a, "s_Xh", [128, KD * TB], BF16)
    s_Xl = mk_sb(g_a, "s_Xl", [128, KD * TB], BF16)
    s_wkh = [mk_sb(g_a, f"s_wkh{i}", [128, D], BF16) for i in range(2)]
    s_wkl = [mk_sb(g_a, f"s_wkl{i}", [128, D], BF16) for i in range(2)]
    s_wvh = [mk_sb(g_a, f"s_wvh{i}", [128, D], BF16) for i in range(2)]
    s_wvl = [mk_sb(g_a, f"s_wvl{i}", [128, D], BF16) for i in range(2)]
    b1 = mk_sb(g_a, "b1", [128, TB]); b2 = mk_sb(g_a, "b2", [128, TB])
    b3 = mk_sb(g_a, "b3", [128, TB])

    p0_v, p0_a = sv.v, sa.v
    wait("sync", sv, p0_v)
    wait("sync", sa, p0_a)
    wait("sync", swx[0], swx[0].v)
    wait("sync", swx[1], swx[1].v)
    wait("sync", swy[0], swy[0].v)
    wait("sync", swy[1], swy[1].v)
    wait("gpsimd", sv, p0_v)
    wait("gpsimd", sa, p0_a)
    wait("gpsimd", swx[0], swx[0].v)
    wait("gpsimd", swx[1], swx[1].v)
    wait("gpsimd", swy[0], swy[0].v)
    wait("gpsimd", swy[1], swy[1].v)
    wait("gpsimd", sx[0], sx[0].v)
    wait("gpsimd", sx[1], sx[1].v)
    wait("gpsimd", qsl[0], qsl[0].v)
    wait("gpsimd", qsl[1], qsl[1].v)

    def combos3(whi, wlo, Xh, Xl):
        return ((whi, Xh), (whi, Xl), (wlo, Xh))

    kv_w_done = [0, 0]
    kv_pe_done = [0, 0]
    a_bank_free = {}
    kvsw_done = 0
    xT_done = 0
    for tc in range(2):
        wait("sync", pe, kv_pe_done[0])
        wait("sync", pe, kv_pe_done[1])
        for kd in range(KD):
            dma("sync", sr, s_Xh[:, kd * TB:(kd + 1) * TB],
                d_xlnh[tc * TB:(tc + 1) * TB, kd * 128:(kd + 1) * 128], transpose=True)
            dma("sync", sr, s_Xl[:, kd * TB:(kd + 1) * TB],
                d_xlnl[tc * TB:(tc + 1) * TB, kd * 128:(kd + 1) * 128], transpose=True)
        xT_done = sr.v
        wait("tensor", sr, xT_done)
        for ke in range(KD):
            p = ke % 2
            wait("gpsimd", pe, kv_pe_done[p])
            for tix in (2 * ke, 2 * ke + 1):
                while len(lo_done_tile.get(tix, [])) < n_lo_per_tile:
                    run_task(1)
                for sem_, val_ in lo_done_tile[tix]:
                    wait("gpsimd", sem_, val_)
            dma("gpsimd", sgk[p], s_wkh[p][:], wk[ke])
            dma("gpsimd", sgk[p], s_wvh[p][:], wv[ke])
            dma("gpsimd", sgk[p], s_wkl[p][:], d_wkl[ke])
            dma("gpsimd", sgk[p], s_wvl[p][:], d_wvl[ke])
            kv_w_done[p] = sgk[p].v
            wait("tensor", sgk[p], kv_w_done[p])
            wait("scalar", sw, kvsw_done)
            for g in range(GB):
                if ("k", g % 2) in a_bank_free:
                    wait("tensor", sa, a_bank_free[("k", g % 2)])
            nmm = 0
            for wt, Xt in combos3(s_wkh[p], s_wkl[p], s_Xh, s_Xl):
                for kd in range(KD):
                    for g in range(GB):
                        nmm += 1
                        st = nmm <= GB
                        sp = nmm > 3 * KD * GB - GB
                        fn = (lambda e, wt=wt, Xt=Xt, kd=kd, g=g, st=st, sp=sp: e.matmul(
                            ps_a[g % 2], wt[:, kd * 128:(kd + 1) * 128],
                            Xt[:, kd * TB + g * NT: kd * TB + g * NT + NT],
                            start=st, stop=sp))
                        if nmm == 3 * KD * GB:
                            op("tensor", pe, fn)
                        else:
                            raw("tensor", fn)
            mm_done = pe.v
            for g in range(GB):
                bank = g % 2
                gs = slice(g * NT, g * NT + NT)
                wait("scalar", pe, mm_done)
                op("scalar", sa, lambda e, bank=bank, gs=gs, ke=ke: e.activation(
                    b1[:, gs], ps_a[bank], ACT.Identity, bias=s_bqkv[:, KD + ke:KD + ke + 1]))
                a_bank_free[("k", bank)] = sa.v
                wait("vector", sa, sa.v)
                op("vector", sv, lambda e, gs=gs: e.tensor_single_scalar(b2[:, gs], b1[:, gs], 0.0, ALU.min))
                wait("scalar", sv, sv.v)
                op("scalar", sa, lambda e, gs=gs: e.activation(b3[:, gs], b2[:, gs], ACT.Exp))
                wait("vector", sa, sa.v)
                op("vector", sv, lambda e, gs=gs: e.scalar_tensor_tensor(
                    b2[:, gs], b1[:, gs], 0.0, b3[:, gs], ALU.max, ALU.add))
            for g in range(GB):
                if ("v", 2 + g % 2) in a_bank_free:
                    wait("tensor", sv, a_bank_free[("v", 2 + g % 2)])
            nmm = 0
            for wt, Xt in combos3(s_wvh[p], s_wvl[p], s_Xh, s_Xl):
                for kd in range(KD):
                    for g in range(GB):
                        nmm += 1
                        st = nmm <= GB
                        sp = nmm > 3 * KD * GB - GB
                        fn = (lambda e, wt=wt, Xt=Xt, kd=kd, g=g, st=st, sp=sp: e.matmul(
                            ps_a[2 + g % 2], wt[:, kd * 128:(kd + 1) * 128],
                            Xt[:, kd * TB + g * NT: kd * TB + g * NT + NT],
                            start=st, stop=sp))
                        if nmm == 3 * KD * GB:
                            op("tensor", pe, fn)
                        else:
                            raw("tensor", fn)
            mm_done = pe.v
            for g in range(GB):
                bank = 2 + (g % 2)
                gs = slice(g * NT, g * NT + NT)
                wait("vector", pe, mm_done)
                op("vector", sv, lambda e, bank=bank, gs=gs, ke=ke: e.scalar_tensor_tensor(
                    b1[:, gs], ps_a[bank], s_bqkv[:, 2 * KD + ke:2 * KD + ke + 1], b2[:, gs],
                    ALU.add, ALU.mult))
                a_bank_free[("v", bank)] = sv.v
            wait("vector", sw, kvsw_done)
            if tc == 0:
                op("vector", sv, lambda e: e.tensor_tensor_scan(
                    b3[:], b1[:], b1[:], 0.0, ALU.add, ALU.bypass))
                op("vector", sv, lambda e, ke=ke: e.tensor_copy(s_carry0[:, ke:ke + 1], b3[:, TB - 1:TB]))
            else:
                op("vector", sv, lambda e, ke=ke: e.tensor_tensor_scan(
                    b3[:], b1[:], b1[:], s_carry0[:, ke:ke + 1], ALU.add, ALU.bypass))
                op("vector", sv, lambda e, ke=ke: e.tensor_copy(s_carry[:, ke:ke + 1], b3[:, TB - 1:TB]))
            scan_v = sv.v
            wait("sync", sv, scan_v)
            dma("sync", sw, d_kvs[ke][:, tc * TB:(tc + 1) * TB], b3[:])
            kvsw_done = sw.v
            kv_pe_done[p] = pe.v
            for _ in range(4):
                lo_task()
            for _ in range(2):
                w1_task()
    a_done_w = sw.v
    a_carry_v = sv.v
    while t_cons[0] < n_lo_tasks + n_w1_tasks:
        if not t_consume_one():
            break
        run_task(0)

    wait("sync", sv, a_carry_v)
    dma("sync", swc, d_carry, s_carry[:])
    wait("gpsimd", swc, swc.v)
    scc.v += 1
    emit("gpsimd", lambda e, h=scc.h: e.collective_compute(
        "AllGather", ALU.bypass, replica_groups=[[0, 1, 2, 3, 4, 5, 6, 7]],
        ins=[d_carry], outs=[d_call]).then_inc(h, 1))
    wait("sync", scc, scc.v)
    for r in range(8):
        dma("sync", src8, s_crecv[:, r * KD:(r + 1) * KD], d_call[r])
    wait("vector", src8, src8.v)
    op("vector", sv, lambda e: e.tensor_scalar(s_rcm[0][:], s_crecv[:, 0:KD], s_psel[:, 0:1], None, ALU.mult))
    for r in range(1, 8):
        cur, nxt = s_rcm[(r - 1) % 2], s_rcm[r % 2]
        op("vector", sv, lambda e, r=r, cur=cur, nxt=nxt: e.scalar_tensor_tensor(
            nxt[:], s_crecv[:, r * KD:(r + 1) * KD], s_psel[:, r:r + 1], cur[:], ALU.mult, ALU.add))
    s_rcmf = s_rcm[7 % 2]
    rcm_done = sv.v

    wait("tensor", sv, sv.v)
    op("tensor", pe, lambda e: e.matmul(ps_b_t[0][0:1, 0:1], s_wsum[:, 0:1], s_one[:], start=True, stop=True))
    op("tensor", pe, lambda e: e.matmul(ps_b_t[0][0:1, 1:2], s_wsum[:, 1:2], s_one[:], start=True, stop=True))
    t_ws = pe.v
    wait("vector", pe, t_ws)
    op("vector", sv, lambda e: e.tensor_copy(s_red[0:1, 0:1], ps_b_t[0][0:1, 0:1]))
    op("vector", sv, lambda e: e.tensor_copy(s_red2[0:1, 0:1], ps_b_t[0][0:1, 1:2]))
    tot_v = sv.v
    wait("tensor", sv, tot_v)
    op("tensor", pe, lambda e: e.matmul(ps_b_t[1][:, 0:1], s_row[:], s_red[0:1, 0:1], start=True, stop=True))
    op("tensor", pe, lambda e: e.matmul(ps_b_t[1][:, 1:2], s_row[:], s_red2[0:1, 0:1], start=True, stop=True))
    t_bc = pe.v
    wait("vector", pe, t_bc)
    op("vector", sv, lambda e: e.tensor_scalar(s_swu[:], ps_b_t[1][:, 0:1], 1.0 / (D * H), None, ALU.mult))
    op("vector", sv, lambda e: e.tensor_scalar(s_swd[:], ps_b_t[1][:, 1:2], 1.0 / (H * D), None, ALU.mult))
    op("vector", sv, lambda e: e.tensor_scalar(s_swu[:], s_swu[:], EPS, None, ALU.max))
    op("vector", sv, lambda e: e.tensor_scalar(s_swd[:], s_swd[:], EPS, None, ALU.max))
    op("vector", sv, lambda e: e.reciprocal(s_iwu[:], s_swu[:]))
    op("vector", sv, lambda e: e.reciprocal(s_iwd[:], s_swd[:]))
    w2_ready[0] = sv.v
    psb_used_v = sv.v

    g_a.close()
    g_b = ExitStack()
    s_Xh2 = mk_sb(g_b, "s_Xh2", [128, KD * TB], BF16)
    s_Xl2 = mk_sb(g_b, "s_Xl2", [128, KD * TB], BF16)

    av_, aa_ = sv.v, sa.v
    wait("sync", sv, av_)
    wait("sync", sa, aa_)
    wait("sync", sgk[0], sgk[0].v)
    wait("sync", sgk[1], sgk[1].v)
    wait("gpsimd", sv, av_)
    wait("gpsimd", sa, aa_)
    wait("gpsimd", pe, pe.v)
    wait("gpsimd", sw, a_done_w)
    wait("gpsimd", sr, sr.v)
    wait("gpsimd", qsl[0], qsl[0].v)
    wait("gpsimd", qsl[1], qsl[1].v)
    wait("tensor", sv, psb_used_v)
    sxb = [Sem("sxb0"), Sem("sxb1")]

    q_w_done = [0, 0]
    q_pe_done = [0, 0]
    b_bank_free = {}
    o_bank_free = {}
    kvt_done = 0
    wo_s_done = 0
    wo_guard = [0, 0]
    r1_guard = 0
    nup_a_done = 0
    nupw_prev = [0, 0]
    r1w_prev = [0, 0]
    xs_done = [0, 0]
    xs_guard = [0, 0]

    s_wqh = [mk_sb(g_b, f"s_wqh{i}", [128, D], BF16) for i in range(2)]
    s_wql = [mk_sb(g_b, f"s_wql{i}", [128, D], BF16) for i in range(2)]
    c1 = mk_sb(g_b, "c1", [128, TB]); c2t = mk_sb(g_b, "c2t", [128, TB])
    c3 = mk_sb(g_b, "c3", [128, TB])

    s_csh = mk_sb(g_b, "s_csh", [128, KD * TS], BF16)
    s_csl = mk_sb(g_b, "s_csl", [128, KD * TS], BF16)
    NTO = NT // 2
    s_woh = [mk_sb(g_b, f"s_woh{i}", [128, KD * NTO], BF16) for i in range(2)]
    s_wol0 = mk_sb(g_b, "s_wol0", [128, KD * NTO], BF16)
    s_wol = [s_wol0, s_wol0]
    s_r1s = [mk_sb(g_b, f"s_r1s{i}", [128, D]) for i in range(TS // 128)]
    s_xs = [mk_sb(g_b, "s_xs0", [128, NTO])] * 2
    s_sq3 = mk_sb(g_b, "s_sq3", [128, D])
    s_nup1 = mk_sb(g_b, "s_nup1", [128, D], BF16)
    _sq3b = s_sq3[:].bitcast(BF16)
    s_cth = _sq3b[:, 0:TB]
    s_ctl = _sq3b[:, TB:2 * TB]
    s_kvb = s_csh[:].bitcast(F32)[:, 0:TB]

    b_pe_mark = 0
    for tc in range(2):
        wait("sync", pe, b_pe_mark)
        for kd in range(KD):
            dma("sync", sr, s_Xh2[:, kd * TB:(kd + 1) * TB],
                d_xlnh[tc * TB:(tc + 1) * TB, kd * 128:(kd + 1) * 128], transpose=True)
            dma("sync", sr, s_Xl2[:, kd * TB:(kd + 1) * TB],
                d_xlnl[tc * TB:(tc + 1) * TB, kd * 128:(kd + 1) * 128], transpose=True)
        xT2_done = sr.v
        wait("tensor", sr, xT2_done)
        wait("vector", pe, pe.v)
        wait("vector", sa, sa.v)
        for ke in range(KD):
            p = ke % 2
            wait("gpsimd", pe, q_pe_done[p])
            while len(lo_done_tile.get(2 * KD + ke, [])) < n_lo_per_tile:
                run_task(1)
            for sem_, val_ in lo_done_tile[2 * KD + ke]:
                wait("gpsimd", sem_, val_)
            dma("gpsimd", sgq[p], s_wqh[p][:], wq[ke])
            dma("gpsimd", sgq[p], s_wql[p][:], d_wql[ke])
            q_w_done[p] = sgq[p].v
            wait("sync", sv, kvt_done)
            if ke == 0:
                wait("sync", pe, pe.v)
                wait("sync", sw, a_done_w)
            dma("sync", srk, s_kvb, d_kvs[ke][:, tc * TB:(tc + 1) * TB])
            kvs_rb = srk.v
            wait("tensor", sgq[p], q_w_done[p])
            wait("scalar", sv, sv.v)
            for g in range(GB):
                if (g % 2) in b_bank_free:
                    wait("tensor", sa, b_bank_free[g % 2])
            nmm = 0
            for wt, Xt in combos3(s_wqh[p], s_wql[p], s_Xh2, s_Xl2):
                for kd in range(KD):
                    for g in range(GB):
                        nmm += 1
                        st = nmm <= GB
                        sp = nmm > 3 * KD * GB - GB
                        fn = (lambda e, wt=wt, Xt=Xt, kd=kd, g=g, st=st, sp=sp: e.matmul(
                            ps_b[g % 2], wt[:, kd * 128:(kd + 1) * 128],
                            Xt[:, kd * TB + g * NT: kd * TB + g * NT + NT],
                            start=st, stop=sp))
                        if nmm == 3 * KD * GB:
                            op("tensor", pe, fn)
                        else:
                            raw("tensor", fn)
            mm_done = pe.v
            for g in range(GB):
                bank = g % 2
                gs = slice(g * NT, g * NT + NT)
                wait("scalar", pe, mm_done)
                op("scalar", sa, lambda e, bank=bank, gs=gs, ke=ke: e.activation(
                    c1[:, gs], ps_b[bank], ACT.Identity, bias=s_bqkv[:, ke:ke + 1]))
                b_bank_free[bank] = sa.v
                wait("vector", sa, sa.v)
                op("vector", sv, lambda e, gs=gs: e.tensor_single_scalar(c2t[:, gs], c1[:, gs], 0.0, ALU.min))
                wait("scalar", sv, sv.v)
                op("scalar", sa, lambda e, gs=gs: e.activation(c3[:, gs], c2t[:, gs], ACT.Exp))
                wait("vector", sa, sa.v)
                op("vector", sv, lambda e, gs=gs: e.scalar_tensor_tensor(
                    c2t[:, gs], c1[:, gs], 0.0, c3[:, gs], ALU.max, ALU.add))
            phiq_done = sv.v
            wait("vector", srk, kvs_rb)
            wait("vector", sv, rcm_done)
            op("vector", sv, lambda e, ke=ke: e.tensor_scalar(
                c1[:], s_kvb, s_rcmf[:, ke:ke + 1], None, ALU.add))
            op("vector", sv, lambda e: e.tensor_tensor(c3[:], c2t[:], c1[:], ALU.mult))
            wait("vector", sch, sch.v)
            wait("vector", scl, scl.v)
            op("vector", sv, lambda e: e.tensor_copy(s_cth, c3[:]))
            op("vector", sv, lambda e: e.tensor_tensor(s_ctl, c3[:], s_cth, ALU.subtract))
            kvt_done = sv.v
            wait("sync", sv, kvt_done)
            dma("sync", sch, d_ctxh[ke][:, tc * TB:(tc + 1) * TB], s_cth)
            dma("sync", scl, d_ctxl[ke][:, tc * TB:(tc + 1) * TB], s_ctl)
            q_pe_done[p] = pe.v
            for _ in range(2):
                w1_task()
        ctx_done = sv.v
        b_pe_mark = pe.v

        wait("tensor", sv, ctx_done)
        wait("vector", sch, sch.v)
        wait("vector", scl, scl.v)
        for sub in range(NSUB):
            t0 = tc * KTB + sub * (TS // 128)
            wait("vector", swr[0], swr[0].v)
            wait("vector", swr[1], swr[1].v)
            wait("sync", sch, sch.v)
            wait("sync", scl, scl.v)
            wait("sync", pe, wo_guard[0])
            for e in range(KD):
                dma("sync", scx, s_csh[:, e * TS:(e + 1) * TS],
                    d_ctxh[e][:, tc * TB + sub * TS: tc * TB + (sub + 1) * TS])
                dma("sync", scx, s_csl[:, e * TS:(e + 1) * TS],
                    d_ctxl[e][:, tc * TB + sub * TS: tc * TB + (sub + 1) * TS])
            ctxs_done = scx.v
            wait("tensor", scx, ctxs_done)
            for eo in range(2 * GE):
                pw = eo % 2
                wait("gpsimd", pe, wo_guard[pw])
                dma("gpsimd", sgo[pw], s_woh[pw][:],
                    wo[:, :, eo * NTO:(eo + 1) * NTO].rearrange("k p n -> p k n"))
                wait("gpsimd", pe, max(wo_guard))
                dma("gpsimd", sgo[pw], s_wol[pw][:],
                    d_wol[:, :, eo * NTO:(eo + 1) * NTO].rearrange("k p n -> p k n"))
                wo_s_done = sgo[pw].v
                wait("tensor", sgo[pw], wo_s_done)
                for ti in range(TS // 128):
                    t = t0 + ti
                    bank = ti % 2
                    if ("o", bank) in o_bank_free:
                        wait("tensor", sv, o_bank_free[("o", bank)])
                    first = True
                    for cstat, wmov in ((s_csh, s_woh[pw]), (s_csl, s_woh[pw]), (s_csh, s_wol[pw])):
                        for e in range(KD):
                            tloc = ti * 128
                            last = (wmov is s_wol[pw]) and e == KD - 1
                            fn = (lambda e_, cstat=cstat, wmov=wmov, e=e, tloc=tloc, bank=bank, st=first, sp=last: e_.matmul(
                                ps_o[bank][:, 0:NTO], cstat[:, e * TS + tloc: e * TS + tloc + 128],
                                wmov[:, e * NTO:(e + 1) * NTO], start=st, stop=sp))
                            if last:
                                op("tensor", pe, fn)
                            else:
                                raw("tensor", fn)
                            first = False
                    mm_done = pe.v
                    pxs = 0
                    wait("sync", sv, xs_guard[pxs])
                    dma("sync", sxs[pxs], s_xs[pxs][:], x[t * 128:(t + 1) * 128, eo * NTO:(eo + 1) * NTO])
                    xs_done[pxs] = sxs[pxs].v
                    wait("gpsimd", sxs[pxs], xs_done[pxs])
                    wait("gpsimd", sv, xs_guard[pxs])
                    dma("gpsimd", sxb[pxs], s_xs[pxs][:], bo_rep[:, eo * NTO:(eo + 1) * NTO], accum=True)
                    es = slice(eo * NTO, eo * NTO + NTO)
                    wait("vector", pe, mm_done)
                    wait("vector", sxb[pxs], sxb[pxs].v)
                    wait("vector", sv, r1_guard)
                    op("vector", sv, lambda e, bank=bank, es=es, ti=ti, pxs=pxs: e.scalar_tensor_tensor(
                        s_r1s[ti][:, es], ps_o[bank][:, 0:NTO], SCALE, s_xs[pxs][:], ALU.mult, ALU.add))
                    o_bank_free[("o", bank)] = sv.v
                    xs_guard[pxs] = sv.v
                wo_guard[pw] = pe.v
            for ti in range(TS // 128):
                t = t0 + ti
                tc2 = slice(t, t + 1)
                wait("vector", sa, nup_a_done)
                op("vector", sv, lambda e, ti=ti: e.tensor_tensor(s_sq3[:], s_r1s[ti][:], s_r1s[ti][:], ALU.mult))
                op("vector", sv, lambda e, tc2=tc2: e.tensor_reduce(s_ssq[:, tc2], s_sq3[:], AX.X, ALU.add))
                op("vector", sv, lambda e, ti=ti, tc2=tc2: e.tensor_reduce(s_sum[:, tc2], s_r1s[ti][:], AX.X, ALU.add))
                op("vector", sv, lambda e, tc2=tc2: e.tensor_scalar(s_sum[:, tc2], s_sum[:, tc2], inv_d, None, ALU.mult))
                op("vector", sv, lambda e, tc2=tc2: e.tensor_scalar(s_ssq[:, tc2], s_ssq[:, tc2], inv_d, None, ALU.mult))
                op("vector", sv, lambda e, tc2=tc2: e.tensor_tensor(s_sc1[:, tc2], s_sum[:, tc2], s_sum[:, tc2], ALU.mult))
                op("vector", sv, lambda e, tc2=tc2: e.tensor_tensor(s_sc1[:, tc2], s_ssq[:, tc2], s_sc1[:, tc2], ALU.subtract))
                op("vector", sv, lambda e, tc2=tc2: e.tensor_scalar(s_sc1[:, tc2], s_sc1[:, tc2], EPS, None, ALU.add))
                op("vector", sv, lambda e, tc2=tc2: e.reciprocal(s_sc2[:, tc2], s_sc1[:, tc2]))
                wait("scalar", sv, sv.v)
                op("scalar", sa, lambda e, tc2=tc2: e.activation(s_rstd[:, tc2], s_sc2[:, tc2], ACT.Sqrt))
                wait("vector", sa, sa.v)
                newton(s_rstd[:, tc2], s_sc1[:, tc2], s_sc2[:, tc2])
                op("vector", sv, lambda e, tc2=tc2: e.tensor_tensor(s_nmr[:, tc2], s_sum[:, tc2], s_rstd[:, tc2], ALU.mult))
                op("vector", sv, lambda e, tc2=tc2: e.tensor_scalar(s_nmr[:, tc2], s_nmr[:, tc2], -1.0, None, ALU.mult))
                wait("scalar", sv, sv.v)
                op("scalar", sa, lambda e, ti=ti, tc2=tc2: e.activation(
                    s_sq3[:], s_r1s[ti][:], ACT.Identity, bias=s_nmr[:, tc2], scale=s_rstd[:, tc2]))
                wait("vector", sa, sa.v)
                op("vector", sv, lambda e, tc2=tc2: e.tensor_reduce(
                    s_c2[:, tc2], s_sq3[:], AX.X, ALU.max, apply_absolute_value=True))
                op("vector", sv, lambda e, tc2=tc2: e.tensor_scalar(s_c2[:, tc2], s_c2[:, tc2], EPS, None, ALU.max))
                op("vector", sv, lambda e, tc2=tc2: e.reciprocal(s_sc2[:, tc2], s_c2[:, tc2]))
                op("vector", sv, lambda e, tc2=tc2: e.tensor_scalar(s_s2[:, tc2], s_sc2[:, tc2], 127.0, None, ALU.mult))
                op("vector", sv, lambda e, tc2=tc2: e.tensor_scalar(
                    s_sq3[:], s_sq3[:], s_s2[:, tc2], MAGIC, ALU.mult, ALU.add))
                wait("scalar", sv, sv.v)
                wait("scalar", swn[0], swn[0].v)
                op("scalar", sa, lambda e: e.activation(s_nup1[:], s_sq3[:], ACT.Identity, bias=s_nmg[:, 0:1]))
                nup_a_done = sa.v
                wait("sync", sa, nup_a_done)
                dma("sync", swn[0], d_nup[t * 128:(t + 1) * 128, :], s_nup1[:])
                wait("sync", sv, sv.v)
                dma("sync", swr[t % 2], d_r1[t * 128:(t + 1) * 128, :], s_r1s[ti][:])
                r1w_prev[t % 2] = swr[t % 2].v
                for _ in range(4):
                    w2_task()
            r1_guard = sv.v

    g_b.close()
    g_up = ExitStack()
    s_nupT = mk_sb(g_up, "s_nupT", [128, KD * T], BF16)
    s_wup = [mk_sb(g_up, f"s_wup{i}", [128, KD * NT], BF16) for i in range(2)]
    s_rel = [mk_sb(g_up, f"s_rel{i}", [128, NT]) for i in range(2)]

    o_v, o_a = sv.v, sa.v
    wait("sync", swn[0], swn[0].v)
    wait("sync", swn[1], swn[1].v)
    wait("sync", swr[0], swr[0].v)
    wait("sync", swr[1], swr[1].v)
    wait("sync", pe, pe.v)
    wait("sync", sv, o_v)
    wait("sync", sa, o_a)
    wait("sync", sxs[0], sxs[0].v)
    wait("sync", sxs[1], sxs[1].v)
    wait("gpsimd", sv, o_v)
    wait("gpsimd", sa, o_a)
    wait("gpsimd", pe, pe.v)
    wait("gpsimd", swr[0], swr[0].v)
    wait("gpsimd", swr[1], swr[1].v)
    for kd in range(KD):
        dma("sync", sr, s_nupT[:, kd * T:(kd + 1) * T],
            d_nup[:, kd * 128:(kd + 1) * 128], transpose=True)
    nupT_done = sr.v
    wait("tensor", sr, nupT_done)
    up_pe_done = [0, 0]
    upw_done = [0, 0]
    up_bank_free = {}
    relw_done = [0, 0]
    up_v_read = [0, 0]
    for h in range(GH):
        p = h % 2
        while t_cons[0] < n_tasks_before_wu:
            if not t_consume_one():
                break
            run_task(0)
        wait("gpsimd", qsu[0], qsu[0].v)
        wait("gpsimd", qsu[1], qsu[1].v)
        wait("gpsimd", pe, up_pe_done[p])
        for kd in range(KD):
            dma("gpsimd", sgu[p], s_wup[p][:, kd * NT:(kd + 1) * NT],
                d_wupq[kd][:, h * NT:(h + 1) * NT])
        upw_done[p] = sgu[p].v
        wait("tensor", sgu[p], upw_done[p])
        for t in range(KT):
            bank = t % 2
            if bank in up_bank_free:
                wait("tensor", sa, up_bank_free[bank])
            for kd in range(KD):
                fn = (lambda e, kd=kd, t=t, p=p, bank=bank: e.matmul(
                    ps_a[bank], s_nupT[:, kd * T + t * 128: kd * T + t * 128 + 128],
                    s_wup[p][:, kd * NT:(kd + 1) * NT],
                    start=(kd == 0), stop=(kd == KD - 1)))
                if kd == KD - 1:
                    op("tensor", pe, fn)
                else:
                    raw("tensor", fn)
            mm_done = pe.v
            wait("scalar", pe, mm_done)
            wait("scalar", swl[bank], relw_done[bank])
            wait("scalar", sv, up_v_read[bank])
            op("scalar", sa, lambda e, bank=bank: e.activation(s_rel[bank][:], ps_a[bank], ACT.Relu))
            up_bank_free[bank] = sa.v
            wait("vector", sa, sa.v)
            wait("vector", sv, sv.v)
            op("vector", sv, lambda e, bank=bank: e.tensor_reduce(s_red[:], s_rel[bank][:], AX.X, ALU.max))
            op("vector", sv, lambda e, t=t: e.tensor_tensor(s_maxr[:, t:t + 1], s_maxr[:, t:t + 1], s_red[:], ALU.max))
            up_v_read[bank] = sv.v
            wait("sync", sa, up_bank_free[bank])
            dma("sync", swl[bank], d_relu[t * 128:(t + 1) * 128, h * NT:(h + 1) * NT], s_rel[bank][:])
            relw_done[bank] = swl[bank].v
        up_pe_done[p] = pe.v
        for _ in range(8):
            w2_task()
    up_done_v = sv.v
    while t_cons[0] < len(tasks):
        if not t_consume_one():
            break
        run_task(0)

    wait("vector", sv, up_done_v)
    op("vector", sv, lambda e: e.tensor_scalar(s_k2[:], s_c2[:], s_swu[:, 0:1], 1.0 / 127.0, ALU.mult, ALU.mult))
    op("vector", sv, lambda e: e.tensor_tensor(s_sc1[:], s_maxr[:], s_k2[:], ALU.mult))
    op("vector", sv, lambda e: e.tensor_tensor(s_sc1[:], s_sc1[:], s_sc1[:], ALU.mult))
    op("vector", sv, lambda e: e.tensor_scalar(s_sc1[:], s_sc1[:], EPS, None, ALU.max))
    op("vector", sv, lambda e: e.reciprocal(s_sc2[:], s_sc1[:]))
    op("vector", sv, lambda e: e.tensor_tensor(s_q3[:], s_k2[:], s_k2[:], ALU.mult))
    op("vector", sv, lambda e: e.tensor_scalar(s_q3[:], s_q3[:], 127.0, None, ALU.mult))
    op("vector", sv, lambda e: e.tensor_tensor(s_q3[:], s_q3[:], s_sc2[:], ALU.mult))
    op("vector", sv, lambda e: e.tensor_scalar(s_k3[:], s_sc1[:], s_swd[:, 0:1], 1.0 / 127.0, ALU.mult, ALU.mult))
    q3_done = sv.v

    g_up.close()
    g_pw.close()
    g_nd = ExitStack()
    s_rlb = [mk_sb(g_nd, f"s_rlb{i}", [128, HH]) for i in range(2)]
    s_sqh = [mk_sb(g_nd, f"s_sqh{i}", [128, HH]) for i in range(2)]
    s_ndn = [mk_sb(g_nd, f"s_ndn{i}", [128, HH], BF16) for i in range(2)]
    g_dn = ExitStack()
    s_ndT = mk_sb(g_dn, "s_ndT", [128, KH * TH], BF16)
    s_wdn = [mk_sb(g_dn, f"s_wdn{i}", [128, KH * NT], BF16) for i in range(2)]
    s_r1L = [mk_sb(g_dn, f"s_r1L{i}", [128, NT]) for i in range(2)]
    s_o = [mk_sb(g_dn, f"s_o{i}", [128, NT]) for i in range(2)]

    up_a = sa.v
    wait("sync", pe, pe.v)
    wait("sync", swl[0], swl[0].v)
    wait("sync", swl[1], swl[1].v)
    wait("sync", sa, up_a)
    wait("sync", sv, sv.v)
    wait("sync", ql[0], ql[0].v)
    wait("sync", ql[1], ql[1].v)
    wait("sync", qsu[0], qsu[0].v)
    wait("sync", qsu[1], qsu[1].v)
    wait("sync", qsd[0], qsd[0].v)
    wait("sync", qsd[1], qsd[1].v)
    wait("sync", qsl[0], qsl[0].v)
    wait("sync", qsl[1], qsl[1].v)
    wait("sync", sgu[0], sgu[0].v)
    wait("sync", sgu[1], sgu[1].v)
    nd_v_done = [0, 0]
    nd_a_done = [0, 0]
    nd_w_done = [0, 0]
    nd_idx = [0]

    nd_issue = [0]
    nd_rb = {}

    def nd_issue_one(seq):
        k = nd_issue[0]
        if k >= len(seq):
            return
        nd_issue[0] += 1
        t, hhf = seq[k]
        pk = k % 2
        wait("sync", sv, nd_v_done[pk])
        wait("sync", sa, nd_a_done[pk])
        dma("sync", srl[pk], s_rlb[pk][:], d_relu[t * 128:(t + 1) * 128, hhf * HH:(hhf + 1) * HH])
        nd_rb[k] = srl[pk].v

    def nd_tiles(trange):
        seq_all = [(t, hhf) for t in range(KT) for hhf in range(2)]
        for t in trange:
            for hhf in range(2):
                k = nd_idx[0]
                p = k % 2
                nd_idx[0] += 1
                while nd_issue[0] <= min(len(seq_all) - 1, k + 1):
                    nd_issue_one(seq_all)
                rb = nd_rb[k]
                wait("vector", srl[p], rb)
                wait("vector", sv, q3_done)
                wait("vector", sa, nd_a_done[p])
                op("vector", sv, lambda e, p=p: e.tensor_tensor(s_sqh[p][:], s_rlb[p][:], s_rlb[p][:], ALU.mult))
                op("vector", sv, lambda e, p=p, t=t: e.tensor_scalar(
                    s_sqh[p][:], s_sqh[p][:], s_q3[:, t:t + 1], MAGIC, ALU.mult, ALU.add))
                nd_v_done[p] = sv.v
                wait("scalar", sv, nd_v_done[p])
                wait("scalar", swd[p], nd_w_done[p])
                op("scalar", sa, lambda e, p=p: e.activation(s_ndn[p][:], s_sqh[p][:], ACT.Identity, bias=s_nmg[:, 0:1]))
                nd_a_done[p] = sa.v
                wait("sync", sa, nd_a_done[p])
                dma("sync", swd[p], d_ndn[t * 128:(t + 1) * 128, hhf * HH:(hhf + 1) * HH], s_ndn[p][:])
                nd_w_done[p] = swd[p].v

    dn_pe_done = [0, 0]
    dn_o_guard = [0, 0]
    dn_r_guard = [0, 0]
    dn_bank_free = {}

    def down_chunk(tcc):
        wait("sync", swd[0], swd[0].v)
        wait("sync", swd[1], swd[1].v)
        wait("sync", swr[0], swr[0].v)
        wait("sync", swr[1], swr[1].v)
        wait("sync", pe, pe.v)
        wait("sync", sv, sv.v)
        wait("sync", sa, sa.v)
        for kh in range(KH):
            dma("sync", sr, s_ndT[:, kh * TH:(kh + 1) * TH],
                d_ndn[tcc * TH:(tcc + 1) * TH, kh * 128:(kh + 1) * 128], transpose=True)
        ndT_done = sr.v
        wait("tensor", sr, ndT_done)
        wait("gpsimd", sv, sv.v)
        wait("gpsimd", sa, sa.v)
        wait("gpsimd", pe, pe.v)
        for e in range(GE):
            pwe = e % 2
            wait("gpsimd", qsd[0], qsd[0].v)
            wait("gpsimd", qsd[1], qsd[1].v)
            wait("gpsimd", pe, dn_pe_done[pwe])
            for kh in range(KH):
                dma("gpsimd", sgd[pwe], s_wdn[pwe][:, kh * NT:(kh + 1) * NT],
                    d_wdnq[kh][:, e * NT:(e + 1) * NT])
            dnw = sgd[pwe].v
            wait("tensor", sgd[pwe], dnw)
            for ti in range(KTH):
                t = tcc * KTH + ti
                bank = 2 + (ti % 2)
                if bank in dn_bank_free:
                    wait("tensor", sv, dn_bank_free[bank])
                for kh in range(KH):
                    fn = (lambda e_, kh=kh, ti=ti, pwe=pwe, bank=bank: e_.matmul(
                        ps_a[bank], s_ndT[:, kh * TH + ti * 128: kh * TH + ti * 128 + 128],
                        s_wdn[pwe][:, kh * NT:(kh + 1) * NT],
                        start=(kh == 0), stop=(kh == KH - 1)))
                    if kh == KH - 1:
                        op("tensor", pe, fn)
                    else:
                        raw("tensor", fn)
                mm_done = pe.v
                pp = ti % 2
                wait("sync", sv, dn_r_guard[pp])
                dma("sync", srr[pp], s_r1L[pp][:], d_r1[t * 128:(t + 1) * 128, e * NT:(e + 1) * NT])
                r1rb = srr[pp].v
                wait("vector", srr[pp], r1rb)
                wait("vector", pe, mm_done)
                wait("vector", so2[pp], dn_o_guard[pp])
                op("vector", sv, lambda e_, bank=bank, pp=pp, t=t: e_.scalar_tensor_tensor(
                    s_o[pp][:], ps_a[bank], s_k3[:, t:t + 1], s_r1L[pp][:], ALU.mult, ALU.add))
                dn_bank_free[bank] = sv.v
                dn_r_guard[pp] = sv.v
                wait("sync", sv, sv.v)
                dma("sync", so2[pp], out[t * 128:(t + 1) * 128, e * NT:(e + 1) * NT], s_o[pp][:])
                dn_o_guard[pp] = so2[pp].v
            dn_pe_done[pwe] = pe.v

    up_pe_total = pe.v
    wait("gpsimd", pe, up_pe_total)
    nd_tiles(range(0, KTH))
    down_chunk(0)
    nd_tiles(range(KTH, KT))
    down_chunk(1)
    wait("sync", so2[0], so2[0].v)
    wait("sync", so2[1], so2[1].v)
    g_dn.close()
    g_nd.close()
    wait("sync", so, so.v)
    g_dn.close()

    with ctx:
        block = ctx.enter_context(nc.Block())

        @block.sync
        def _(eng):
            for f in plan["sync"]:
                f(eng)

        @block.gpsimd
        def _(eng):
            for f in plan["gpsimd"]:
                f(eng)

        @block.vector
        def _(eng):
            for f in plan["vector"]:
                f(eng)

        @block.scalar
        def _(eng):
            for f in plan["scalar"]:
                f(eng)

        @block.tensor
        def _(eng):
            for f in plan["tensor"]:
                f(eng)

    return nc


def _lay_proj(W):
    E, D = W.shape
    KD = D // 128
    return np.ascontiguousarray(W.T.reshape(KD, 128, E))


def _lay_stat(W):
    E, D = W.shape
    KE, KD = E // 128, D // 128
    A = W.reshape(KE, 128, KD, 128)
    return np.ascontiguousarray(A.transpose(0, 3, 2, 1).reshape(KE, 128, D))


def make_inputs(x_full, Wq, bq, Wk, bk, Wv, bv, Wo, bo, w_up, w_down, T):
    D = Wq.shape[0]
    KD = D // 128
    wq_l = _lay_stat(np.asarray(Wq, np.float32))
    wk_l = _lay_stat(np.asarray(Wk, np.float32))
    wv_l = _lay_stat(np.asarray(Wv, np.float32))
    wo_l = _lay_proj(np.asarray(Wo, np.float32))
    wup_l = _lay_proj(np.asarray(w_up, np.float32))
    wdn_l = _lay_proj(np.asarray(w_down, np.float32))
    bqkv = np.concatenate([np.asarray(bq, np.float32).reshape(KD, 128).T,
                           np.asarray(bk, np.float32).reshape(KD, 128).T,
                           np.asarray(bv, np.float32).reshape(KD, 128).T], axis=1)
    bo_r = np.ascontiguousarray(np.broadcast_to(
        np.asarray(bo, np.float32)[None, :], (128, D)))
    maps = []
    for c in range(8):
        ps = np.zeros((128, 8), np.float32)
        if c % 2 == 1:
            ps[:, c - 1] = 1.0
        maps.append({
            "x": np.ascontiguousarray(x_full[c * T:(c + 1) * T]),
            "wq": wq_l, "wk": wk_l, "wv": wv_l, "wo": wo_l,
            "bqkv": np.ascontiguousarray(bqkv), "bo_rep": bo_r,
            "wup": wup_l, "wdn": wdn_l, "psel": ps,
        })
    return maps


def kernel(L0_attn_x, L0_attn_Wq, L0_attn_bq, L0_attn_Wk, L0_attn_bk,
           L0_attn_Wv, L0_attn_bv, L0_attn_Wo, L0_attn_bo,
           L0_ffn_w_up, L0_ffn_w_down):
    B, L, D = L0_attn_x.shape
    T = (B * L) // 8
    H = L0_ffn_w_up.shape[0]
    x_full = np.asarray(L0_attn_x, dtype=np.float32).reshape(B * L, D)
    nc = build_nc(D, T, H, NT=512)
    in_maps = make_inputs(x_full, L0_attn_Wq, L0_attn_bq, L0_attn_Wk, L0_attn_bk,
                          L0_attn_Wv, L0_attn_bv, L0_attn_Wo, L0_attn_bo,
                          L0_ffn_w_up, L0_ffn_w_down, T)
    res = run_bass_kernel_spmd(nc, in_maps, core_ids=list(range(8)))
    outs = [np.asarray(res.results[c]["out"]) for c in range(8)]
    return np.concatenate(outs, axis=0).reshape(B, L, D).astype(np.float32)
